# version 11
# baseline (speedup 1.0000x reference)
import numpy as np
import ml_dtypes
from contextlib import ExitStack

import concourse.bass as bass
import concourse.mybir as mybir
from concourse.bass_utils import run_bass_kernel_spmd

F32 = mybir.dt.float32
BF16 = mybir.dt.bfloat16
FP16 = mybir.dt.float16
I32 = mybir.dt.int32
AF = mybir.ActivationFunctionType
ALU = mybir.AluOpType
AX = mybir.AxisListType.X

BSTEP = 99
NSCRUN = NSC_DEFAULT = 8
S12SUB = 9
S9SUB = 9
T = 4096
D = 1024
H = 16
HD = 64
FD = 64
DFF = 4096
SC = 512
NSC = T // SC
NTT = T // 128


def load_act_set(sc_eng, set_id):
    inst = mybir.InstLoadActFuncSet(act_func_set_id=set_id,
                                    name=f"actload_{set_id}_{load_act_set.k}",
                                    engine=mybir.EngineType.Activation)
    load_act_set.k += 1
    sc_eng.add_instruction(inst)


load_act_set.k = 0


class Ctr:

    def __init__(self, stack, nc, name):
        self.h = stack.enter_context(nc.semaphore(name))
        self.n = 0

    def inc(self, instr, k=1):
        instr.then_inc(self.h, k)
        self.n += k
        return instr

    def dinc(self, instr):
        return self.inc(instr, 16)

    def wait(self, eng, lag=0):
        eng.wait_ge(self.h, max(self.n - lag, 0))

    def wait_abs(self, eng, val):
        eng.wait_ge(self.h, val)


def host_consts():
    eye_f = np.eye(128, dtype=np.float32)
    eye_b = np.eye(128, dtype=ml_dtypes.bfloat16)
    tri = np.triu(np.ones((128, 128), dtype=np.float32)).astype(ml_dtypes.bfloat16)
    m16 = np.zeros((128, 8 * 16), dtype=np.float32)
    for c in range(8):
        m16[0:64, 16 * c + 2 * c] = 1.0
        m16[64:128, 16 * c + 2 * c + 1] = 1.0
    m16 = m16.astype(ml_dtypes.bfloat16)
    ihp = np.zeros((16, 8 * 128), dtype=np.float32)
    for hp in range(8):
        ihp[2 * hp, 128 * hp: 128 * hp + 64] = 1.0
        ihp[2 * hp + 1, 128 * hp + 64: 128 * hp + 128] = 1.0
    ihp = ihp.astype(np.float16)
    sel2 = np.zeros((2, 128), dtype=np.float32)
    p = np.arange(128)
    sel2[0, (p % 64) // 32 == 0] = 1.0
    sel2[1, (p % 64) // 32 == 1] = 1.0
    return dict(eye_f=eye_f, eye_b=eye_b, tri=tri, m16=m16, ihp=ihp, sel2=sel2)


def build_nc(dbg=(), phases=4):
    nc = bass.Bass("TRN2", target_bir_lowering=False, debug=False)
    dbg = set(dbg)

    x = nc.dram_tensor("x", [T, D], F32, kind="ExternalInput").ap()
    wqkv = nc.dram_tensor("wqkv", [D, 3 * D], F32, kind="ExternalInput").ap()
    bqkv = nc.dram_tensor("bqkv", [3 * D], F32, kind="ExternalInput").ap()
    wout = nc.dram_tensor("wout", [D, D], F32, kind="ExternalInput").ap()
    bout = nc.dram_tensor("bout", [D], F32, kind="ExternalInput").ap()
    omega_t = nc.dram_tensor("omega_t", [H * HD, FD], F32, kind="ExternalInput").ap()
    qn = nc.dram_tensor("qn", [2, 1], F32, kind="ExternalInput").ap()
    qw = nc.dram_tensor("qw", [2, 1], F32, kind="ExternalInput").ap()
    win_t = nc.dram_tensor("win_t", [DFF, D], F32, kind="ExternalInput").ap()
    b_in = nc.dram_tensor("b_in", [DFF], F32, kind="ExternalInput").ap()
    wofn = nc.dram_tensor("wofn", [DFF, D], F32, kind="ExternalInput").ap()
    b_out = nc.dram_tensor("b_out", [D], F32, kind="ExternalInput").ap()
    eye_f = nc.dram_tensor("eye_f", [128, 128], F32, kind="ExternalInput").ap()
    eye_b = nc.dram_tensor("eye_b", [128, 128], BF16, kind="ExternalInput").ap()
    tri = nc.dram_tensor("tri", [128, 128], BF16, kind="ExternalInput").ap()
    m16 = nc.dram_tensor("m16", [128, 128], BF16, kind="ExternalInput").ap()
    ihp = nc.dram_tensor("ihp", [16, 1024], FP16, kind="ExternalInput").ap()
    sel2 = nc.dram_tensor("sel2", [2, 128], F32, kind="ExternalInput").ap()

    out = nc.dram_tensor("out", [T, D], F32, kind="ExternalOutput").ap()

    h_spill = nc.dram_tensor("h_spill", [DFF, T], BF16).ap()
    ctx_spill = nc.dram_tensor("ctx_spill", [D, T], BF16).ap()
    wos_spill = nc.dram_tensor("wos_spill", [DFF, D], BF16).ap()
    ccxi = nc.dram_tensor("ccxi", [1, 8], F32).ap()
    ccxo = nc.dram_tensor("ccxo", [1, 8], F32).ap()
    cchi = nc.dram_tensor("cchi", [1, 8], F32).ap()
    ccho = nc.dram_tensor("ccho", [1, 8], F32).ap()

    dbg_out = {}

    def dbg_tensor(name, shape, dt):
        if name in dbg:
            dbg_out[name] = nc.dram_tensor("dbg_" + name, shape, dt, kind="ExternalOutput").ap()

    dbg_tensor("xn", [128, NTT * 8 * 128], BF16)
    dbg_tensor("qkfm", [128, 16 * SC], BF16)
    dbg_tensor("qf", [128, 8 * SC], BF16)
    dbg_tensor("kf", [128, 8 * SC], BF16)
    dbg_tensor("vplus", [128, 4 * 1040], BF16)
    dbg_tensor("ctx", [128, 4 * 1024], BF16)
    dbg_tensor("ctx1", [128, 4 * 1024], BF16)
    dbg_tensor("amx", [1, 8], F32)
    dbg_tensor("amxh", [1, 8], F32)
    dbg_tensor("hrow", [128, T], BF16)
    dbg_tensor("wis", [128, 8 * DFF], BF16)
    dbg_tensor("wos", [128, 32 * D], BF16)

    with ExitStack() as st_:
        e = st_.enter_context

        def C(name):
            return Ctr(st_, nc, name)

        dXr = [C("dXr0"), C("dXr1"), C("dXr2"), C("dXr3"), C("dXr4"), C("dXr5")]
        dK = C("dK");  dW = C("dW");  dWr2 = [C("dWr0"), C("dWr1")]; dWS = C("dWS")
        dOr2 = [C("dOr0"), C("dOr1")]; dIr2 = [C("dIr0"), C("dIr1")]
        dSP = C("dSP"); dRL = C("dRL"); dST = C("dST"); dCC = C("dCC"); dCC2 = C("dCC2")
        cCC = C("cCC"); cCC2 = C("cCC2")
        tC = C("tC");  tA = C("tA");  tQ = C("tQ");  tF = C("tF");  tT = C("tT")
        tAT = C("tAT"); tCX = C("tCX"); tUP = C("tUP"); tTR = C("tTR")
        tW = C("tW");  tH = C("tH");  tO = C("tO");  tR = C("tR")
        vC = C("vC");  vA = C("vA");  vF = C("vF");  vM = C("vM");  vX = C("vX")
        vQ = C("vQ");  vW = C("vW");  vH = C("vH");  vD = C("vD");  vO = C("vO")
        sC = C("sC");  sA = C("sA");  sQ = C("sQ");  sF = C("sF");  sS = C("sS")
        sW = C("sW");  sH = C("sH");  sD = C("sD")
        gF = C("gF")


        eyef_sb = e(nc.sbuf_tensor("eyef_sb", [128, 128], F32))
        zero_col = e(nc.sbuf_tensor("zero_col", [128, 1], F32))
        eps_col = e(nc.sbuf_tensor("eps_col", [128, 1], F32))
        ones_row = e(nc.sbuf_tensor("ones_row", [1, 128], F32))
        s_col = e(nc.sbuf_tensor("s_col", [128, 1], F32))
        negs_col = e(nc.sbuf_tensor("negs_col", [128, 1], F32))
        sq2_col = e(nc.sbuf_tensor("sq2_col", [128, 1], F32))
        c2_col = e(nc.sbuf_tensor("c2_col", [128, 1], F32))
        lnc2_col = e(nc.sbuf_tensor("lnc2_col", [128, 1], F32))
        bin_sb = e(nc.sbuf_tensor("bin_sb", [128, 32], F32))
        bcomb = e(nc.sbuf_tensor("bcomb", [128, 8], F32))
        bo1_sb = e(nc.sbuf_tensor("bo1_sb", [128, 8], F32))
        bo2_sb = e(nc.sbuf_tensor("bo2_sb", [128, 8], F32))
        amax_sc = e(nc.sbuf_tensor("amax_sc", [1, 8], F32))
        glob_sc = e(nc.sbuf_tensor("glob_sc", [1, 8], F32))
        globh_sc = e(nc.sbuf_tensor("globh_sc", [1, 8], F32))
        sxi = e(nc.sbuf_tensor("sxi", [128, 8], F32))
        eyeb_sb = e(nc.sbuf_tensor("eyeb_sb", [128, 128], BF16))
        ps = [e(nc.psum_tensor(f"ps{i}", [128, 512], F32)) for i in range(6)]
        psb = [e(nc.psum_tensor(f"psb{i}", [128, 1024], BF16)) for i in range(2)]

        gXN = ExitStack()
        xn = gXN.enter_context(nc.sbuf_tensor("xn", [128, NTT * 8 * 128], BF16))
        amax_cols = gXN.enter_context(nc.sbuf_tensor("amax_cols", [128, 40], F32))

        gBc = ExitStack()
        wq_sb = gBc.enter_context(nc.sbuf_tensor("wq_sb", [128, 8 * 3 * D], BF16))
        omg = gBc.enter_context(nc.sbuf_tensor("omg", [128, 8 * FD], BF16))
        tri_sb = gBc.enter_context(nc.sbuf_tensor("tri_sb", [128, 128], BF16))
        m16_sb = gBc.enter_context(nc.sbuf_tensor("m16_sb", [128, 128], BF16))
        ihpq_sb = gBc.enter_context(nc.sbuf_tensor("ihpq_sb", [16, 1024], FP16))
        bq_sb = gBc.enter_context(nc.sbuf_tensor("bq_sb", [128, 16], F32))
        BV = gBc.enter_context(nc.sbuf_tensor("BV", [128, D], F32))

        gP0 = ExitStack()
        ihp_sb = gP0.enter_context(nc.sbuf_tensor("ihp_sb", [16, 1024], FP16))
        sel2_sb = gP0.enter_context(nc.sbuf_tensor("sel2_sb", [2, 128], F32))
        qn_sb = gP0.enter_context(nc.sbuf_tensor("qn_sb", [2, 1], F32))
        qw_sb = gP0.enter_context(nc.sbuf_tensor("qw_sb", [2, 1], F32))
        bv_row = gP0.enter_context(nc.sbuf_tensor("bv_row", [1, D], F32))
        sq2_row = gP0.enter_context(nc.sbuf_tensor("sq2_row", [1, 128], F32))
        sq16 = gP0.enter_context(nc.sbuf_tensor("sq16", [16, 128], F32))

        blk = e(nc.Block())

        mk = {}
        MOFF = [0, 512, 896, 1152]

        @blk.sync
        def _(sync):
            for dst, src in [(eyef_sb, eye_f), (eyeb_sb, eye_b), (tri_sb, tri),
                             (m16_sb, m16), (ihp_sb, ihp), (sel2_sb, sel2),
                             (qn_sb, qn), (qw_sb, qw)]:
                dK.dinc(sync.dma_start(out=dst[:], in_=src[:, :]))
            dK.dinc(sync.dma_start(out=bq_sb[:],
                                   in_=bqkv[0:2048].rearrange("(c p) -> p c", p=128),
                                   allow_slow_non_contiguous=True))
            dK.dinc(sync.dma_start(out=bv_row[:],
                                   in_=bqkv[2048:3072].rearrange("(a c) -> a c", a=1)))
            dK.dinc(sync.dma_start(out=bin_sb[:],
                                   in_=b_in.rearrange("(c p) -> p c", p=128),
                                   allow_slow_non_contiguous=True))
            dK.dinc(sync.dma_start(out=bo1_sb[:],
                                   in_=bout.rearrange("(c p) -> p c", p=128),
                                   allow_slow_non_contiguous=True))
            dK.dinc(sync.dma_start(out=bo2_sb[:],
                                   in_=b_out.rearrange("(c p) -> p c", p=128),
                                   allow_slow_non_contiguous=True))

        @blk.gpsimd
        def _(g):
            dW.dinc(g.dma_start(out=omg.ap().rearrange("p (r m) -> p r m", m=FD),
                                in_=omega_t.rearrange("(r u d) m -> (u d) r m", u=2, d=64)))

        @blk.vector
        def _(v):
            vC.inc(v.memset(zero_col[:], 0.0))
            vC.inc(v.memset(eps_col[:], 1e-5))
            vC.inc(v.memset(ones_row[:], 1.0))
            vC.inc(v.memset(amax_sc[:], 0.0))

        @blk.tensor
        def _(t):
            dK.wait(t)
            t.matmul(ps[0][:, 0:1], sel2_sb[:, :], qn_sb[:, :], start=True, stop=True)
            tC.inc(t.matmul(ps[0][:, 4:5], sel2_sb[:, :], qw_sb[:, :], start=True, stop=True))

        @blk.vector
        def _(v):
            tC.wait(v)
            vC.inc(v.tensor_copy(s_col[:], ps[0][:, 0:1]))
            vC.inc(v.tensor_scalar(out=negs_col[:], in0=ps[0][:, 0:1], scalar1=-1.0,
                                   scalar2=None, op0=ALU.mult))
            vC.inc(v.tensor_scalar(out=sxi[:, 7:8], in0=ps[0][:, 0:1], scalar1=0.0,
                                   scalar2=2.0, op0=ALU.max, op1=ALU.mult))
            vC.inc(v.tensor_scalar(out=c2_col[:], in0=ps[0][:, 4:5], scalar1=0.0,
                                   scalar2=1.0 / 32.0, op0=ALU.max, op1=ALU.mult))

        @blk.scalar
        def _(s):
            vC.wait(s)
            load_act_set(s, 3)
            sC.inc(s.activation(out=sq2_col[:], in_=sxi[:, 7:8], func=AF.Sqrt,
                                bias=zero_col[:], scale=1.0))
            load_act_set(s, 6)
            sC.inc(s.activation(out=lnc2_col[:], in_=c2_col[:], func=AF.Ln,
                                bias=zero_col[:], scale=1.0))

        @blk.tensor
        def _(t):
            sC.wait(t)
            tC.inc(t.transpose(ps[1][0:1, 0:128], sq2_col[:, 0:1], eyef_sb[:]))

        @blk.vector
        def _(v):
            tC.wait(v)
            vC.inc(v.tensor_copy(sq2_row[:], ps[1][0:1, 0:128]))

        @blk.tensor
        def _(t):
            vC.wait(t)
            t.matmul(ps[1][0:16, 0:128], ones_row[0:1, 0:16], sq2_row[0:1, :],
                     start=True, stop=True)
            t.matmul(ps[2][:, 0:512], ones_row[0:1, :], bv_row[0:1, 0:512],
                     start=True, stop=True)
            tC.inc(t.matmul(ps[3][:, 0:512], ones_row[0:1, :], bv_row[0:1, 512:1024],
                            start=True, stop=True))

        @blk.vector
        def _(v):
            tC.wait(v)
            vC.inc(v.tensor_copy(sq16[:], ps[1][0:16, 0:128]))
            vC.inc(v.tensor_copy(BV[:, 0:512], ps[2][:, 0:512]))
            vC.inc(v.tensor_copy(BV[:, 512:1024], ps[3][:, 0:512]))
            vC.wait(v)
            for hp in range(8):
                vC.inc(v.tensor_tensor(out=ihpq_sb[:, 128 * hp:128 * (hp + 1)],
                                       in0=ihp_sb[:, 128 * hp:128 * (hp + 1)],
                                       in1=sq16[:, :], op=ALU.mult))
            vC.inc(v.tensor_tensor(out=bcomb[:], in0=bo1_sb[:], in1=bo2_sb[:], op=ALU.add))

        gP0.close()

        gA = ExitStack()
        wqstg = gA.enter_context(nc.sbuf_tensor("wqstg", [128, 2 * 3 * D], F32))
        wstgA = gA.enter_context(nc.sbuf_tensor("wstgA", [128, 2 * D], F32))
        wamaxA = gA.enter_context(nc.sbuf_tensor("wamaxA", [128, 8], F32))
        itileA = gA.enter_context(nc.sbuf_tensor("itileA", [128, 2 * D], I32))
        wsimA = gA.enter_context(nc.sbuf_tensor("wsimA", [128, 2 * D], BF16))

        for k8 in range(8):
            @blk.gpsimd
            def _(g, k8=k8):
                if k8 == 0:
                    tC.wait(g)
                    vC.wait(g)
                    sC.wait(g)
                if k8 >= 2:
                    vC.wait_abs(g, mk[("wqc", k8 - 2)])
                dWr2[k8 % 2].dinc(g.dma_start(
                    out=wqstg[:, 3 * D * (k8 % 2):3 * D * (k8 % 2 + 1)],
                    in_=wqkv[128 * k8:128 * (k8 + 1), :]))
                mk[("wql", k8)] = dWr2[k8 % 2].n

            @blk.vector
            def _(v, k8=k8):
                dWr2[k8 % 2].wait_abs(v, mk[("wql", k8)])
                vC.inc(v.tensor_copy(wq_sb[:, 3 * D * k8:3 * D * (k8 + 1)],
                                     wqstg[:, 3 * D * (k8 % 2):3 * D * (k8 % 2 + 1)]))
                mk[("wqc", k8)] = vC.n


        xt = gA.enter_context(nc.sbuf_tensor("xt", [128, 3 * D], F32))
        bn6 = gA.enter_context(nc.sbuf_tensor("bn6", [128, 24], F32))
        mv2 = gA.enter_context(nc.sbuf_tensor("mv2", [128, 4], F32))
        rstd = gA.enter_context(nc.sbuf_tensor("rstd", [128, 4], F32))
        xn_tm = gA.enter_context(nc.sbuf_tensor("xn_tm", [128, 2 * D], BF16))

        ts_done = {}
        for i in range(NTT):
            sl = i % 3
            nsl = i % 2

            def _xload(eng, j):
                if j <= 1:
                    tC.wait(eng)
                    vC.wait(eng)
                    sC.wait(eng)
                if j >= 3:
                    vA.wait_abs(eng, ts_done[j - 3])
                dXr[j % 6].dinc(eng.dma_start(
                    out=xt[:, D * (j % 3):D * (j % 3 + 1)],
                    in_=x[128 * j:128 * (j + 1), :]))

            if i == 0:
                @blk.sync
                def _(sync):
                    _xload(sync, 0)

                @blk.scalar
                def _(s):
                    _xload(s, 1)

                @blk.sync
                def _(sync):
                    _xload(sync, 2)
            elif i + 2 < NTT:
                if (i + 2) % 2 == 0:
                    @blk.sync
                    def _(sync, i=i):
                        _xload(sync, i + 2)
                else:
                    @blk.scalar
                    def _(s, i=i):
                        _xload(s, i + 2)

            def _bn(v, j):
                dXr[j % 6].wait(v)
                if j >= 2:
                    sA.wait_abs(v, mk[("lnex", j - 2)])
                xs = xt[:, D * (j % 3):D * (j % 3 + 1)]
                par = j % 2
                vA.inc(v.bn_stats(bn6[:, 12 * par:12 * par + 6], xs[:, 0:512]))
                vA.inc(v.bn_stats(bn6[:, 12 * par + 6:12 * par + 12], xs[:, 512:1024]))
                vA.wait(v)
                vA.inc(v.bn_aggr(mv2[:, 2 * par:2 * (par + 1)], bn6[:, 12 * par:12 * (par + 1)]))
                mk[("aggr", j)] = vA.n

            if i == 0:
                @blk.vector
                def _(v):
                    _bn(v, 0)

            @blk.scalar
            def _(s, i=i):
                vA.wait_abs(s, mk[("aggr", i)])
                par = i % 2
                sA.inc(s.activation(out=rstd[:, 2 * par:2 * par + 1],
                                    in_=mv2[:, 2 * par + 1:2 * par + 2], func=AF.Ln,
                                    bias=eps_col[:], scale=1.0))
                sA.wait(s)
                sA.inc(s.activation(out=rstd[:, 2 * par + 1:2 * par + 2],
                                    in_=rstd[:, 2 * par:2 * par + 1], func=AF.Exp,
                                    bias=zero_col[:], scale=-0.5))
                mk[("lnex", i)] = sA.n

            if i + 1 < NTT:
                @blk.vector
                def _(v, i=i):
                    _bn(v, i + 1)

            @blk.vector
            def _(v, i=i, sl=sl, nsl=nsl):
                sA.wait_abs(v, mk[("lnex", i)])
                if i >= 2:
                    tA.wait_abs(v, i - 1)
                par = i % 2
                xs = xt[:, D * sl:D * (sl + 1)]
                xn_t = xn_tm[:, D * nsl:D * (nsl + 1)]
                vA.inc(v.tensor_scalar(out=xn_t, in0=xs, scalar1=mv2[:, 2 * par:2 * par + 1],
                                       scalar2=rstd[:, 2 * par + 1:2 * par + 2],
                                       op0=ALU.subtract, op1=ALU.mult))
                ts_done[i] = vA.n
                vA.wait(v)
                vA.inc(v.tensor_reduce(out=amax_cols[:, i:i + 1], in_=xn_t, axis=AX,
                                       op=ALU.max, apply_absolute_value=True))

            @blk.tensor
            def _(t, i=i, nsl=nsl):
                vA.wait(t)
                if i == 0:
                    vC.wait(t)
                if i >= 2:
                    sA.wait_abs(t, mk[("xnev", i - 2)])
                xn_t = xn_tm[:, D * nsl:D * (nsl + 1)]
                pb = psb[i % 2]
                last = None
                for c in range(8):
                    last = t.transpose(pb[:, 128 * c:128 * (c + 1)],
                                       xn_t[:, 128 * c:128 * (c + 1)], eyeb_sb[:])
                tA.inc(last)

            @blk.scalar
            def _(s, i=i):
                tA.wait(s)
                pb = psb[i % 2]
                dst = xn.ap().rearrange("p (c a) -> p c a", a=NTT * 128)[
                    :, :, 128 * i:128 * (i + 1)]
                sA.inc(s.activation(out=dst,
                                    in_=pb.ap().rearrange("p (c a) -> p c a", a=128),
                                    func=AF.Identity, bias=zero_col[:], scale=1.0))
                mk[("xnev", i)] = sA.n

            @blk.gpsimd
            def _(g, jc=i):
                if jc >= 2:
                    sW.wait_abs(g, mk[("oi2", jc - 2)])
                dOr2[jc % 2].dinc(g.dma_start(
                    out=wstgA[:, D * (jc % 2):D * (jc % 2 + 1)],
                    in_=wofn[128 * jc:128 * (jc + 1), :]))
                mk[("old", jc)] = dOr2[jc % 2].n

            @blk.vector
            def _(v, jc=i):
                sl = jc % 2
                dOr2[sl].wait_abs(v, mk[("old", jc)])
                vW.inc(v.tensor_reduce(out=wamaxA[:, sl:sl + 1],
                                       in_=wstgA[:, D * sl:D * (sl + 1)], axis=AX,
                                       op=ALU.max, apply_absolute_value=True))
                vW.wait(v)
                vW.inc(v.tensor_scalar(out=wamaxA[:, 6 + sl:7 + sl],
                                       in0=wamaxA[:, sl:sl + 1], scalar1=1e-8,
                                       scalar2=1.0 / 127.0, op0=ALU.max, op1=ALU.mult))
                vW.wait(v)
                vW.inc(v.reciprocal(wamaxA[:, 2 + sl:3 + sl], wamaxA[:, 6 + sl:7 + sl]))
                mk[("osc", jc)] = vW.n

            @blk.scalar
            def _(s, jc=i):
                sl = jc % 2
                vW.wait_abs(s, mk[("osc", jc)])
                if jc >= 2:
                    sW.wait_abs(s, mk[("oi2", jc - 2)])
                sW.inc(s.activation(out=itileA[:, D * sl:D * (sl + 1)],
                                    in_=wstgA[:, D * sl:D * (sl + 1)],
                                    func=AF.Identity, bias=zero_col[:],
                                    scale=wamaxA[:, 2 + sl:3 + sl]))
                mk[("oi1", jc)] = sW.n
                sW.wait(s)
                if jc >= 2:
                    dWS.wait_abs(s, mk[("osp", jc - 2)])
                sW.inc(s.activation(out=wsimA[:, D * sl:D * (sl + 1)],
                                    in_=itileA[:, D * sl:D * (sl + 1)],
                                    func=AF.Identity, bias=zero_col[:],
                                    scale=wamaxA[:, 6 + sl:7 + sl]))
                mk[("oi2", jc)] = sW.n

            @blk.gpsimd
            def _(g, jc=i):
                sl = jc % 2
                sW.wait_abs(g, mk[("oi2", jc)])
                dWS.wait(g)
                dWS.dinc(g.dma_start(out=wos_spill[128 * jc:128 * (jc + 1), :],
                                     in_=wsimA[:, D * sl:D * (sl + 1)]))
                mk[("osp", jc)] = dWS.n

        @blk.vector
        def _(v):
            vA.wait(v)
            vA.inc(v.tensor_reduce(out=amax_cols[:, 36:37], in_=amax_cols[:, 0:32],
                                   axis=AX, op=ALU.max))

        @blk.tensor
        def _(t):
            vA.wait(t)
            tA.inc(t.transpose(ps[5][0:1, 256:384], amax_cols[:, 36:37], eyef_sb[:]))

        @blk.vector
        def _(v):
            tA.wait(v)
            vA.inc(v.tensor_reduce(out=amax_sc[0:1, 0:1], in_=ps[5][0:1, 256:384],
                                   axis=AX, op=ALU.max))
            mk["amaxrd"] = vA.n

        @blk.gpsimd
        def _(g):
            vA.wait(g)
            dCC.dinc(g.dma_start(out=ccxi[0:1, :], in_=amax_sc[0:1, :]))
            dCC.wait(g)
            cCC.inc(g.collective_compute("AllReduce", ALU.max,
                                         replica_groups=[list(range(8))],
                                         ins=[ccxi.opt()], outs=[ccxo.opt()]))
            cCC.wait(g)
            dCC.dinc(g.dma_start(out=glob_sc[0:1, :], in_=ccxo[0:1, :]))

        gA.close()

        gB = ExitStack()
        qk_fm = gB.enter_context(nc.sbuf_tensor("qk_fm", [128, 16 * SC], BF16))
        vplus = gB.enter_context(nc.sbuf_tensor("vplus", [128, 4 * 1040], BF16))
        q2b = gB.enter_context(nc.sbuf_tensor("q2b", [128, 2 * SC], BF16))
        Ssb = gB.enter_context(nc.sbuf_tensor("Ssb", [16, 2 * SC], F32))
        nqk = gB.enter_context(nc.sbuf_tensor("nqk", [16, 2 * SC], FP16))
        n2sb = gB.enter_context(nc.sbuf_tensor("n2sb", [128, 2 * SC], FP16))
        argb = gB.enter_context(nc.sbuf_tensor("argb", [128, 2 * SC], F32))
        qf = gB.enter_context(nc.sbuf_tensor("qf", [128, 8 * SC], BF16))
        kf = gB.enter_context(nc.sbuf_tensor("kf", [128, 8 * SC], BF16))
        kf_tm = gB.enter_context(nc.sbuf_tensor("kf_tm", [128, 4 * 1024], BF16))
        stp = gB.enter_context(nc.sbuf_tensor("stp", [128, 8 * 65], BF16))
        mat = gB.enter_context(nc.sbuf_tensor("mat", [128, 1280], BF16))
        Ksb = gB.enter_context(nc.sbuf_tensor("Ksb", [64, 16], F32))
        S32 = gB.enter_context(nc.sbuf_tensor("S32", [64, 1024], F32))
        dcol = gB.enter_context(nc.sbuf_tensor("dcol", [128, 4], F32))
        rcol = gB.enter_context(nc.sbuf_tensor("rcol", [128, 4], F32))
        ctx_tm = gB.enter_context(nc.sbuf_tensor("ctx_tm", [128, 4 * 1024], BF16))
        ctxfm = gB.enter_context(nc.sbuf_tensor("ctxfm", [128, 8 * SC], BF16))

        bQKt = C("bQKt"); bQKs = C("bQKs"); bVt = C("bVt"); bVs = C("bVs")
        bSQs = C("bSQs"); bNMt = C("bNMt"); bNSv = C("bNSv"); bNs = C("bNs")
        bPJt = C("bPJt"); bN2t = C("bN2t"); bN2s = C("bN2s"); bTTv = C("bTTv")
        bCLg = C("bCLg"); bXPs = C("bXPs"); bC2v = C("bC2v")
        bKTt = C("bKTt"); bKTs = C("bKTs"); bSPs = C("bSPs")
        bATt = C("bATt"); bMDv = C("bMDv"); bMOs = C("bMOs")
        bCXt = C("bCXt"); bDEv = C("bDEv"); bUPt = C("bUPt")
        bTRt = C("bTRt"); bTRs = C("bTRs"); bKDv = C("bKDv"); bSTg = C("bSTg")



        @blk.vector
        def _(v):
            if BSTEP < 2:
                return
            tA.wait(v)
            sA.wait(v)
            vW.wait(v)
            dSP.wait(v)
            dWS.wait(v)
            dWr2[0].wait(v)
            dWr2[1].wait(v)
            dOr2[0].wait(v)
            dOr2[1].wait(v)
            vC.inc(v.memset(
                vplus.ap().rearrange("p (t h c) -> p t h c", h=16, c=65)[:, :, :, 64:65], 1.0))
            vC.inc(v.memset(stp[:], 0.0))
            vC.inc(v.memset(Ksb[:], 0.0))

        for sc in range(min(NSC, NSCRUN) if BSTEP >= 2 else 0):
            for m in range(16):
                @blk.tensor
                def _(t, sc=sc, m=m):
                    if m == 0:
                        sA.wait_abs(t, mk[("xnev", min(4 * sc + 3, NTT - 1))])
                        if sc == 0:
                            dW.wait(t)
                            vC.wait(t)
                        elif BSTEP >= 7:
                            bDEv.wait_abs(t, mk[("den", sc - 1, 15)])
                            bMDv.wait_abs(t, mk[("mask", sc - 1, 15)][0])
                            bMOs.wait_abs(t, mk[("mask", sc - 1, 15)][1])
                        elif BSTEP >= 5:
                            bTTv.wait_abs(t, mk[("tt", sc - 1, 15)])
                    bk = ps[m % 2]
                    if m >= 2:
                        bQKs.wait_abs(t, mk[("qke", sc, m - 2)])
                    for k in range(8):
                        ins = t.matmul(bk[:, 0:512],
                                       wq_sb[:, 3 * D * k + 128 * m:3 * D * k + 128 * (m + 1)],
                                       xn[:, 4096 * k + SC * sc:4096 * k + SC * (sc + 1)],
                                       start=(k == 0), stop=(k == 7))
                    bQKt.inc(ins)
                    mk[("qkm", sc, m)] = bQKt.n

                @blk.scalar
                def _(s, sc=sc, m=m):
                    bk = ps[m % 2]
                    bQKt.wait_abs(s, mk[("qkm", sc, m)])
                    if sc > 0 and m == 0 and BSTEP >= 5:
                        bPJt.wait_abs(s, mk[("pjall", sc - 1)])
                    bQKs.inc(s.activation(out=qk_fm[:, SC * m:SC * (m + 1)], in_=bk[:, 0:512],
                                          func=AF.Identity, bias=bq_sb[:, m:m + 1], scale=1.0))
                    mk[("qke", sc, m)] = bQKs.n

            for g8 in range(8 if BSTEP >= 3 else 0):
                @blk.tensor
                def _(t, sc=sc, g8=g8):
                    ts4, g = g8 // 2, g8 % 2
                    bk = ps[2 + g8 % 2]
                    if sc > 0 and g8 < 2 and BSTEP >= 5:
                        bN2s.wait_abs(t, mk[("n2c", sc - 1, 14 + g8)])
                    if (sc, g8) not in [(0, 0), (0, 1)]:
                        key = ("ve", sc if g8 >= 2 else sc - 1, g8 - 2 if g8 >= 2 else 6 + g8)
                        bVs.wait_abs(t, mk[key])
                    for k in range(8):
                        ins = t.matmul(bk[:, 0:512],
                                       xn[:, 4096 * k + SC * sc + 128 * ts4:4096 * k + SC * sc + 128 * (ts4 + 1)],
                                       wq_sb[:, 3 * D * k + 2048 + 512 * g:3 * D * k + 2048 + 512 * (g + 1)],
                                       start=(k == 0), stop=(k == 7))
                    bVt.inc(ins)
                    mk[("vm", sc, g8)] = bVt.n

                @blk.scalar
                def _(s, sc=sc, g8=g8):
                    ts4, g = g8 // 2, g8 % 2
                    bk = ps[2 + g8 % 2]
                    bVt.wait_abs(s, mk[("vm", sc, g8)])
                    if sc > 0 and g8 == 0 and BSTEP >= 7:
                        bCXt.wait_abs(s, mk[("cxall", sc - 1)])
                        bUPt.wait_abs(s, mk[("upall", sc - 1)])
                    if sc == 0 and g8 == 0:
                        vC.wait(s)
                    dst = vplus.ap().rearrange("p (t h c) -> p t h c", h=16, c=65)[
                        :, ts4, 8 * g:8 * (g + 1), 0:64]
                    bVs.inc(s.activation(out=dst,
                                         in_=bk.ap().rearrange("p (h c) -> p h c", c=64),
                                         func=AF.Identity, bias=zero_col[:], scale=1.0))
                    mk[("ve", sc, g8)] = bVs.n

            for c in range(16 if BSTEP >= 4 else 0):
                @blk.scalar
                def _(s, sc=sc, c=c):
                    if c >= 2:
                        bNMt.wait_abs(s, mk[("nm", sc, c - 2)])
                    bSQs.inc(s.activation(out=q2b[:, SC * (c % 2):SC * (c % 2 + 1)],
                                          in_=qk_fm[:, SC * c:SC * (c + 1)],
                                          func=AF.Square, bias=zero_col[:], scale=1.0))
                    mk[("sq", sc, c)] = bSQs.n

                @blk.tensor
                def _(t, sc=sc, c=c):
                    if c == 0:
                        bVs.wait_abs(t, mk[("ve", sc, 7)])
                        if sc > 0:
                            bNSv.wait_abs(t, mk[("nsv", sc - 1)])
                    bk = ps[2] if c < 8 else ps[3]
                    bSQs.wait_abs(t, mk[("sq", sc, c)])
                    cc = c % 8
                    ins = t.matmul(bk[0:16, 0:512], m16_sb[:, 16 * cc:16 * (cc + 1)],
                                   q2b[:, SC * (c % 2):SC * (c % 2 + 1)],
                                   start=(cc == 0), stop=(cc == 7))
                    bNMt.inc(ins)
                    mk[("nm", sc, c)] = bNMt.n

            @blk.vector
            def _(v, sc=sc):
                if BSTEP < 4:
                    return
                bNMt.wait_abs(v, mk[("nm", sc, 7)])
                bNSv.inc(v.tensor_scalar(out=Ssb[0:16, 0:512], in0=ps[2][0:16, 0:512],
                                         scalar1=1e-24, scalar2=None, op0=ALU.max))
                bNMt.wait_abs(v, mk[("nm", sc, 15)])
                bNSv.inc(v.tensor_scalar(out=Ssb[0:16, 512:1024], in0=ps[3][0:16, 0:512],
                                         scalar1=1e-24, scalar2=None, op0=ALU.max))
                mk[("nsv", sc)] = bNSv.n

            @blk.scalar
            def _(s, sc=sc):
                if BSTEP < 4:
                    return
                bNSv.wait_abs(s, mk[("nsv", sc)])
                bNs.inc(s.activation(out=argb[0:16, 0:1024], in_=Ssb[0:16, 0:1024],
                                     func=AF.Ln, bias=zero_col[0:16, :], scale=1.0))
                bNs.wait(s)
                bNs.inc(s.activation(out=nqk[0:16, 0:1024], in_=argb[0:16, 0:1024],
                                     func=AF.Exp, bias=zero_col[0:16, :], scale=-0.5))
                mk[("n", sc)] = bNs.n

            for u16 in range(16 if BSTEP >= 5 else 0):
                isq = u16 < 8
                hp = u16 % 8
                pjbank = ps[u16 % 2]
                n2bank = ps[2 + u16 % 2]
                n2slot = u16 % 2
                argslot = u16 % 2

                @blk.tensor
                def _(t, sc=sc, u16=u16, isq=isq, hp=hp, pjbank=pjbank, n2bank=n2bank):
                    m = hp if isq else 8 + hp
                    bQKs.wait_abs(t, mk[("qke", sc, m)])
                    if (sc, u16) not in [(0, 0), (0, 1)]:
                        key = ("ttp", sc, u16 - 2) if u16 >= 2 else ("tt", sc - 1, 14 + u16)
                        bTTv.wait_abs(t, mk[key])
                    for u in range(2):
                        base = 64 * u
                        ins = t.matmul(pjbank[base:base + 64, 0:512],
                                       omg[base:base + 64, 64 * hp:64 * (hp + 1)],
                                       qk_fm[base:base + 64, SC * m:SC * (m + 1)],
                                       start=True, stop=True,
                                       tile_position=(base, base))
                    bPJt.inc(ins)
                    mk[("pj", sc, u16)] = bPJt.n
                    if u16 == 15:
                        mk[("pjall", sc)] = bPJt.n
                    if S9SUB < 2:
                        mk[("n2", sc, u16)] = bN2t.n
                        return
                    bNs.wait_abs(t, mk[("n", sc)])
                    if (sc, u16) not in [(0, 0), (0, 1)]:
                        key = ("n2c", sc, u16 - 2) if u16 >= 2 else ("n2c", sc - 1, 14 + u16)
                        bN2s.wait_abs(t, mk[key])
                    off = 0 if isq else 512
                    bN2t.inc(t.matmul(n2bank[:, 0:512], ihpq_sb[:, 128 * hp:128 * (hp + 1)],
                                      nqk[0:16, off:off + 512], start=True, stop=True))
                    mk[("n2", sc, u16)] = bN2t.n

                @blk.scalar
                def _(s, sc=sc, u16=u16, n2bank=n2bank, n2slot=n2slot):
                    if S9SUB < 3:
                        mk[("n2c", sc, u16)] = bN2s.n
                        return
                    bN2t.wait_abs(s, mk[("n2", sc, u16)])
                    if (sc, u16) not in [(0, 0), (0, 1)]:
                        key = ("ttp", sc, u16 - 2) if u16 >= 2 else ("tt", sc - 1, 14 + u16)
                        bTTv.wait_abs(s, mk[key])
                    bN2s.inc(s.activation(out=n2sb[:, SC * n2slot:SC * (n2slot + 1)],
                                          in_=n2bank[:, 0:512], func=AF.Identity,
                                          bias=zero_col[:], scale=1.0))
                    mk[("n2c", sc, u16)] = bN2s.n

                @blk.vector
                def _(v, sc=sc, u16=u16, pjbank=pjbank, n2slot=n2slot, argslot=argslot):
                    if S9SUB < 4:
                        mk[("tt", sc, u16)] = bTTv.n
                        return
                    bN2s.wait_abs(v, mk[("n2c", sc, u16)])
                    bPJt.wait_abs(v, mk[("pj", sc, u16)])
                    if (sc, u16) not in [(0, 0), (0, 1)]:
                        if u16 >= 2:
                            if S9SUB >= 6:
                                bXPs.wait_abs(v, mk[("xp", sc, u16 - 2)])
                            elif S9SUB >= 5:
                                bCLg.wait_abs(v, mk[("clip", sc, u16 - 2)])
                        else:
                            if S9SUB >= 6:
                                bXPs.wait_abs(v, mk[("xp", sc - 1, 14 + u16)])
                            elif S9SUB >= 5:
                                bCLg.wait_abs(v, mk[("clip", sc - 1, 14 + u16)])
                    a = argb[:, SC * argslot:SC * (argslot + 1)]
                    bTTv.inc(v.tensor_tensor(out=a, in0=pjbank[:, 0:512],
                                             in1=n2sb[:, SC * n2slot:SC * (n2slot + 1)],
                                             op=ALU.mult))
                    mk[("ttp", sc, u16)] = bTTv.n
                    bTTv.wait(v)
                    bTTv.inc(v.tensor_scalar(out=a, in0=a, scalar1=negs_col[:, 0:1],
                                             scalar2=20.0, op0=ALU.add, op1=ALU.min))
                    bTTv.wait(v)
                    bTTv.inc(v.tensor_scalar(out=a, in0=a, scalar1=-20.0, scalar2=None,
                                             op0=ALU.max))
                    mk[("tt", sc, u16)] = bTTv.n
                    mk[("clip", sc, u16)] = bCLg.n

                def _xp(s, sc=sc, ux=None):
                    isqx = ux < 8
                    hpx = ux % 8
                    if S9SUB < 6:
                        mk[("xp", sc, ux)] = bXPs.n
                        return
                    bTTv.wait_abs(s, mk[("tt", sc, ux)])
                    dstt = qf if isqx else kf
                    if sc > 0 and ux == 0 and BSTEP >= 7:
                        bATt.wait_abs(s, mk[("atall", sc - 1)])
                        bCXt.wait_abs(s, mk[("cxall", sc - 1)])
                    if sc > 0 and ux == 8 and BSTEP >= 6:
                        bKTt.wait_abs(s, mk[("ktall", sc - 1)])
                    bXPs.inc(s.activation(out=dstt[:, SC * hpx:SC * (hpx + 1)],
                                          in_=argb[:, SC * (ux % 2):SC * (ux % 2 + 1)],
                                          func=AF.Exp,
                                          bias=zero_col[:] if isqx else lnc2_col[:],
                                          scale=1.0))
                    mk[("xp", sc, ux)] = bXPs.n

                if u16 >= 1:
                    @blk.scalar
                    def _(s, sc=sc, u16=u16, _xp=_xp):
                        _xp(s, sc=sc, ux=u16 - 1)
                if u16 == 15:
                    @blk.scalar
                    def _(s, sc=sc, _xp=_xp):
                        _xp(s, sc=sc, ux=15)

                if not isq:
                    mk[("c2", sc, hp)] = ("xp", sc, u16)

            for hp in range(8 if BSTEP >= 6 else 0):
                @blk.tensor
                def _(t, sc=sc, hp=hp):
                    bXPs.wait_abs(t, mk[mk[("c2", sc, hp)]])
                    pb = psb[hp % 2]
                    if sc > 0 and hp < 2 and BSTEP >= 8:
                        bTRs.wait_abs(t, mk[("tre", sc - 1, 2 + hp)])
                    if (sc, hp) not in [(0, 0), (0, 1)]:
                        key = ("kte", sc, hp - 2) if hp >= 2 else ("kte", sc - 1, 6 + hp)
                        bKTs.wait_abs(t, mk[key])
                    last = None
                    for ts4 in range(4):
                        last = t.transpose(pb[:, 128 * ts4:128 * (ts4 + 1)],
                                           kf[:, SC * hp + 128 * ts4:SC * hp + 128 * (ts4 + 1)],
                                           eyeb_sb[:])
                    bKTt.inc(last)
                    mk[("kt", sc, hp)] = bKTt.n
                    if hp == 7:
                        mk[("ktall", sc)] = bKTt.n

                @blk.scalar
                def _(s, sc=sc, hp=hp):
                    bKTt.wait_abs(s, mk[("kt", sc, hp)])
                    if sc > 0 and hp == 0 and BSTEP >= 7:
                        bUPt.wait_abs(s, mk[("upall", sc - 1)])
                    pb = psb[hp % 2]
                    dst = kf_tm.ap().rearrange("p (t c) -> p t c", c=1024)[
                        :, :, 128 * hp:128 * (hp + 1)]
                    bKTs.inc(s.activation(out=dst,
                                          in_=pb.ap().rearrange("p (t c) -> p t c", c=128)[:, 0:4, :],
                                          func=AF.Identity, bias=zero_col[:], scale=1.0))
                    mk[("kte", sc, hp)] = bKTs.n

            @blk.scalar
            def _(s, sc=sc):
                if BSTEP < 7:
                    return
                if sc > 0:
                    vH.wait_abs(s, mk[("s32", sc - 1, 0)])
                    vH.wait_abs(s, mk[("s32", sc - 1, 1)])
                    bKDv.wait_abs(s, mk[("kdall", sc - 1)])
                    bCXt.wait_abs(s, mk[("cxall", sc - 1)])
                    bSPs.inc(s.activation(
                        out=stp.ap().rearrange("p (b c) -> p b c", c=65)[0:64, :, 0:64],
                        in_=S32.ap().rearrange("p (b c) -> p b c", c=64)[:, 0:8, :],
                        func=AF.Identity, bias=zero_col[0:64, :], scale=1.0))
                    bSPs.inc(s.activation(
                        out=stp.ap().rearrange("p (b c) -> p b c", c=65)[0:64, :, 64:65],
                        in_=Ksb.ap().rearrange("p (b c) -> p b c", c=2)[:, :, 0:1],
                        func=AF.Identity, bias=zero_col[0:64, :], scale=1.0))
                mk[("stpcp", sc)] = bSPs.n

            @blk.gpsimd
            def _(g, sc=sc):
                if BSTEP < 7:
                    return
                if sc > 0:
                    vH.wait_abs(g, mk[("s32", sc - 1, 1)])
                    bKDv.wait_abs(g, mk[("kdall", sc - 1)])
                    bCXt.wait_abs(g, mk[("cxall", sc - 1)])
                    bSTg.dinc(g.dma_start(
                        out=stp.ap().rearrange("p (b c) -> p b c", c=65)[64:128, :, 0:64],
                        in_=S32.ap().rearrange("p (b c) -> p b c", c=64)[:, 8:16, :]))
                    bSTg.dinc(g.dma_start(
                        out=stp.ap().rearrange("p (b c) -> p b c", c=65)[64:128, :, 64:65],
                        in_=Ksb.ap().rearrange("p (b c) -> p b c", c=2)[:, :, 1:2],
                        allow_slow_non_contiguous=True))
                mk[("stg", sc)] = bSTg.n

            @blk.scalar
            def _(s, sc=sc):
                if BSTEP < 7:
                    mk[("stp", sc)] = bSPs.n
                    return
                if sc > 0:
                    bSTg.wait_abs(s, mk[("stg", sc)])
                    bSPs.inc(s.nop())
                mk[("stp", sc)] = bSPs.n

            for h in range(16 if BSTEP >= 7 else 0):
                hp, u = h // 2, h % 2
                base = 64 * u
                atbank = ps[2 + h % 2]
                cxbank = ps[h % 2]

                for kc in range(4):
                    @blk.tensor
                    def _(t, sc=sc, h=h, hp=hp, u=u, base=base, atbank=atbank, kc=kc):
                        if kc == 0:
                            bXPs.wait_abs(t, mk[("xp", sc, hp)])
                            bXPs.wait_abs(t, mk[mk[("c2", sc, hp)]])
                            if h < 2:
                                bN2s.wait_abs(t, mk[("n2c", sc, 14 + h)])
                            if (sc, h) != (0, 0):
                                keyc = ("cx", sc, h - 1) if h >= 1 else ("cx", sc - 1, 15)
                                bCXt.wait_abs(t, mk[keyc])
                        prev = (sc, h, kc - 2)
                        if kc >= 2:
                            bMDv.wait_abs(t, mk[("maskd1", sc, h, kc - 2)])
                            if kc - 2 < 3:
                                bMOs.wait_abs(t, mk[("masko1", sc, h, kc - 2)])
                        elif (sc, h) not in [(0, 0), (0, 1)]:
                            key = (sc, h - 2) if h >= 2 else (sc - 1, 14 + h)
                            bMDv.wait_abs(t, mk[("maskd1", key[0], key[1], kc + 2)])
                            if kc + 2 < 3:
                                bMOs.wait_abs(t, mk[("masko1", key[0], key[1], kc + 2)])
                            bMDv.wait_abs(t, mk[("maskd1", key[0], key[1], kc)])
                            if kc < 3:
                                bMOs.wait_abs(t, mk[("masko1", key[0], key[1], kc)])
                        nkc = 512 - 128 * kc
                        bATt.inc(t.matmul(
                            atbank[:, 0:nkc] if kc % 2 == 0 else ps[2 + (h + 1) % 2][:, 0:nkc],
                            kf[base:base + 64, SC * hp + 128 * kc:SC * hp + 128 * (kc + 1)],
                            qf[base:base + 64, SC * hp + 128 * kc:SC * (hp + 1)],
                            start=True, stop=True, tile_position=(base, 0)))
                        mk[("at", sc, h, kc)] = bATt.n
                        if h == 15 and kc == 3:
                            mk[("atall", sc)] = bATt.n

                    @blk.vector
                    def _(v, sc=sc, h=h, atbank=atbank, kc=kc):
                        bATt.wait_abs(v, mk[("at", sc, h, kc)])
                        bk = atbank if kc % 2 == 0 else ps[2 + (h + 1) % 2]
                        bMDv.inc(v.tensor_tensor(out=mat[:, MOFF[kc]:MOFF[kc] + 128],
                                                 in0=bk[:, 0:128], in1=tri_sb[:],
                                                 op=ALU.mult))
                        mk[("maskd1", sc, h, kc)] = bMDv.n

                    @blk.scalar
                    def _(s, sc=sc, h=h, atbank=atbank, kc=kc):
                        if kc < 3:
                            nkc = 512 - 128 * kc
                            bATt.wait_abs(s, mk[("at", sc, h, kc)])
                            bMDv.wait_abs(s, mk[("maskd1", sc, h, kc)])
                            bk = atbank if kc % 2 == 0 else ps[2 + (h + 1) % 2]
                            bMOs.inc(s.activation(out=mat[:, MOFF[kc] + 128:MOFF[kc] + nkc],
                                                  in_=bk[:, 128:nkc], func=AF.Identity,
                                                  bias=zero_col[:], scale=1.0))
                            mk[("masko1", sc, h, kc)] = bMOs.n
                        if kc == 3:
                            mk[("mask", sc, h)] = (bMDv.n, bMOs.n)

                @blk.tensor
                def _(t, sc=sc, h=h, hp=hp, u=u, base=base, cxbank=cxbank):
                    if S12SUB < 2:
                        mk[("cx", sc, h)] = bCXt.n
                        if h == 15:
                            mk[("cxall", sc)] = bCXt.n
                        return
                    bMDv.wait_abs(t, mk[("mask", sc, h)][0])
                    bMOs.wait_abs(t, mk[("mask", sc, h)][1])
                    bSPs.wait_abs(t, mk[("stp", sc)])
                    if h < 2:
                        bTTv.wait_abs(t, mk[("ttp", sc, 14 + h)])
                    if (sc, h) not in [(0, 0), (0, 1)]:
                        key = ("den", sc, h - 2) if h >= 2 else ("den", sc - 1, 14 + h)
                        bDEv.wait_abs(t, mk[key])
                    bKTs.wait_abs(t, mk[("kte", sc, hp)])
                    for qs in range(4):
                        if qs == 3:
                            for kc in range(4):
                                bCXt.inc(t.matmul(
                                    cxbank[0:64, 260:261],
                                    kf_tm[:, 1024 * kc + 128 * hp + base:1024 * kc + 128 * hp + base + 64],
                                    vplus[:, 1040 * kc + 65 * h + 64:1040 * kc + 65 * (h + 1)],
                                    start=False, stop=False))
                        for kc in range(qs + 1):
                            last = (qs == 3 and ((sc == 0 and kc == 3)))
                            bCXt.inc(t.matmul(
                                cxbank[:, 65 * qs:65 * (qs + 1)],
                                mat[:, MOFF[kc] + 128 * (qs - kc):MOFF[kc] + 128 * (qs - kc + 1)],
                                vplus[:, 1040 * kc + 65 * h:1040 * kc + 65 * (h + 1)],
                                start=(qs == 0 and kc == 0), stop=last))
                        if sc > 0:
                            bCXt.inc(t.matmul(
                                cxbank[:, 65 * qs:65 * (qs + 1)],
                                qf[base:base + 64, SC * hp + 128 * qs:SC * hp + 128 * (qs + 1)],
                                stp[base:base + 64, 65 * hp:65 * (hp + 1)],
                                start=False, stop=(qs == 3), tile_position=(base, 0)))
                    mk[("cx", sc, h)] = bCXt.n
                    if h == 15:
                        mk[("cxall", sc)] = bCXt.n

                @blk.vector
                def _(v, sc=sc, h=h, hp=hp, u=u, cxbank=cxbank):
                    if S12SUB < 2:
                        mk[("den", sc, h)] = bDEv.n
                        if h == 15:
                            mk[("kdall", sc)] = bKDv.n
                        return
                    bCXt.wait_abs(v, mk[("cx", sc, h)])
                    if sc > 0:
                        bKDv.inc(v.tensor_tensor(out=Ksb[:, 2 * hp + u:2 * hp + u + 1],
                                                 in0=Ksb[:, 2 * hp + u:2 * hp + u + 1],
                                                 in1=cxbank[0:64, 260:261], op=ALU.add))
                    else:
                        bKDv.inc(v.tensor_copy(Ksb[:, 2 * hp + u:2 * hp + u + 1],
                                               cxbank[0:64, 260:261]))
                    if h == 15:
                        mk[("kdall", sc)] = bKDv.n
                    bDEv.inc(v.tensor_scalar(
                        out=dcol.ap().rearrange("p (q c) -> p q c", c=1),
                        in0=cxbank.ap()[:, 0:260].rearrange("p (q c) -> p q c", c=65)[:, :, 64:65],
                        scalar1=1e-6, scalar2=None, op0=ALU.max))
                    bDEv.wait(v)
                    bDEv.inc(v.reciprocal(rcol[:, 0:4], dcol[:, 0:4]))
                    bDEv.wait(v)
                    for qs in range(4):
                        bDEv.inc(v.scalar_tensor_tensor(
                            out=ctx_tm[:, 1024 * qs + 64 * h:1024 * qs + 64 * (h + 1)],
                            in0=cxbank[:, 65 * qs:65 * qs + 64], scalar=rcol[:, qs:qs + 1],
                            in1=BV[:, 64 * h:64 * (h + 1)], op0=ALU.mult, op1=ALU.add))
                    mk[("den", sc, h)] = bDEv.n

                @blk.tensor
                def _(t, sc=sc, h=h, hp=hp, u=u, base=base):
                    if S12SUB < 3:
                        mk[("up", sc, h)] = bUPt.n
                        if h == 15:
                            mk[("upall", sc)] = bUPt.n
                        return
                    sbank = ps[4] if u == 0 else ps[5]
                    if sc == 0 and h == 1:
                        vA.wait_abs(t, mk["amaxrd"])
                    if h == u and sc > 0:
                        vH.wait_abs(t, mk[("s32", sc - 1, u)])
                    for kc in range(4):
                        bUPt.inc(t.matmul(
                            sbank[0:64, 64 * hp:64 * (hp + 1)],
                            kf_tm[:, 1024 * kc + 128 * hp + base:1024 * kc + 128 * hp + base + 64],
                            vplus[:, 1040 * kc + 65 * h:1040 * kc + 65 * h + 64],
                            start=(h == u and kc == 0), stop=(h == 14 + u and kc == 3)))
                    mk[("up", sc, h)] = bUPt.n
                    if h == 15:
                        mk[("upall", sc)] = bUPt.n

                if h >= 14:
                    @blk.vector
                    def _(v, sc=sc, h=h, u=h - 14):
                        if S12SUB < 3:
                            mk[("s32", sc, u)] = vH.n
                            return
                        bUPt.wait_abs(v, mk[("up", sc, h)])
                        sbank = ps[4] if u == 0 else ps[5]
                        if sc == 0:
                            vH.inc(v.tensor_copy(S32[:, 512 * u:512 * (u + 1)],
                                                 sbank[0:64, 0:512]))
                        else:
                            vH.inc(v.tensor_tensor(out=S32[:, 512 * u:512 * (u + 1)],
                                                   in0=S32[:, 512 * u:512 * (u + 1)],
                                                   in1=sbank[0:64, 0:512], op=ALU.add))
                        mk[("s32", sc, u)] = vH.n

            for qs in range(4 if BSTEP >= 8 else 0):
                @blk.tensor
                def _(t, sc=sc, qs=qs):
                    bDEv.wait_abs(t, mk[("den", sc, 15)])
                    bKTs.wait_abs(t, mk[("kte", sc, 7)])
                    pb = psb[qs % 2]
                    if (sc, qs) not in [(0, 0), (0, 1)]:
                        key = ("tre", sc, qs - 2) if qs >= 2 else ("tre", sc - 1, 2 + qs)
                        bTRs.wait_abs(t, mk[key])
                    last = None
                    for c in range(8):
                        last = t.transpose(pb[:, 128 * c:128 * (c + 1)],
                                           ctx_tm[:, 1024 * qs + 128 * c:1024 * qs + 128 * (c + 1)],
                                           eyeb_sb[:])
                    bTRt.inc(last)
                    mk[("tr", sc, qs)] = bTRt.n

                @blk.scalar
                def _(s, sc=sc, qs=qs):
                    bTRt.wait_abs(s, mk[("tr", sc, qs)])
                    if sc > 0 and qs == 0:
                        dSP.wait(s)
                    pb = psb[qs % 2]
                    dst = ctxfm.ap().rearrange("p (c w) -> p c w", w=512)[
                        :, :, 128 * qs:128 * (qs + 1)]
                    bTRs.inc(s.activation(out=dst,
                                          in_=pb.ap().rearrange("p (c w) -> p c w", w=128),
                                          func=AF.Identity, bias=zero_col[:], scale=1.0))
                    mk[("tre", sc, qs)] = bTRs.n

            @blk.sync
            def _(sync, sc=sc):
                if BSTEP < 8:
                    return
                bTRs.wait_abs(sync, mk[("tre", sc, 3)])
                dSP.wait(sync)
                dSP.dinc(sync.dma_start(
                    out=ctx_spill.rearrange("(c p) t -> p c t", p=128)[:, :, SC * sc:SC * (sc + 1)],
                    in_=ctxfm.ap().rearrange("p (c w) -> p c w", w=512)))

            mk[("trall", sc)] = None

        if "qkfm" in dbg_out and BSTEP >= 2:
            @blk.sync
            def _(sync):
                bQKs.wait(sync)
                dST.dinc(sync.dma_start(out=dbg_out["qkfm"][:, :], in_=qk_fm[:]))
        if "qf" in dbg_out and BSTEP >= 5:
            @blk.sync
            def _(sync):
                bXPs.wait(sync)
                dST.dinc(sync.dma_start(out=dbg_out["qf"][:, :], in_=qf[:]))
        if "kf" in dbg_out and BSTEP >= 5:
            @blk.sync
            def _(sync):
                bC2v.wait(sync)
                dST.dinc(sync.dma_start(out=dbg_out["kf"][:, :], in_=kf[:]))
        if "vplus" in dbg_out and BSTEP >= 3:
            @blk.sync
            def _(sync):
                bVs.wait(sync)
                dST.dinc(sync.dma_start(out=dbg_out["vplus"][:, :], in_=vplus[:]))
        if "ctx" in dbg_out and BSTEP >= 7:
            @blk.sync
            def _(sync):
                bDEv.wait(sync)
                dST.dinc(sync.dma_start(out=dbg_out["ctx"][:, :], in_=ctx_tm[:]))

        gB.close()
        gBc.close()

        gC = ExitStack()
        wstg = gC.enter_context(nc.sbuf_tensor("wstg", [128, 2 * D], F32))
        wamax = gC.enter_context(nc.sbuf_tensor("wamax", [128, 8], F32))
        itile = gC.enter_context(nc.sbuf_tensor("itile", [128, 2 * D], I32))
        wsimt = gC.enter_context(nc.sbuf_tensor("wsimt", [128, 2 * D], BF16))
        win_sim = gC.enter_context(nc.sbuf_tensor("win_sim", [128, 8 * DFF], BF16))
        hstg = gC.enter_context(nc.sbuf_tensor("hstg", [128, 2 * T], BF16))
        hamax_cols = gC.enter_context(nc.sbuf_tensor("hamax_cols", [128, 40], F32))

        cWQs = sW; cWQv = vW; cWTt = tW; cWEs = sQ; cXQs = sF; cXQv = vQ
        cSXv = vX; cHt = tH; cHs = sH; cHD = dSP; cAMv = vH; cAMt = tA

        if BSTEP >= 9:
            @blk.tensor
            def _(t):
                dCC.wait(t)
                tC.inc(t.matmul(ps[0][:, 0:1], ones_row[0:1, :], glob_sc[0:1, 0:1],
                                start=True, stop=True))
                mk["sxbc"] = tC.n

            @blk.vector
            def _(v):
                tC.wait_abs(v, mk["sxbc"])
                cSXv.inc(v.tensor_scalar(out=sxi[:, 4:5], in0=ps[0][:, 0:1],
                                         scalar1=1e-8, scalar2=None, op0=ALU.max))
                cSXv.wait(v)
                cSXv.inc(v.reciprocal(sxi[:, 5:6], sxi[:, 4:5]))
                cSXv.wait(v)
                cSXv.inc(v.tensor_scalar(out=sxi[:, 0:1], in0=sxi[:, 5:6], scalar1=32767.0,
                                         scalar2=None, op0=ALU.mult))
                cSXv.inc(v.tensor_scalar(out=sxi[:, 1:2], in0=sxi[:, 4:5],
                                         scalar1=1.0 / 32767.0, scalar2=None, op0=ALU.mult))
                mk["sx"] = cSXv.n

            def _c2piece(pc):
                kc, q4 = pc // 4, pc % 4

                @blk.scalar
                def _(s, pc=pc, kc=kc, q4=q4):
                    if pc == 0:
                        load_act_set(s, 10)
                    cSXv.wait_abs(s, mk["sx"])
                    if pc >= 2:
                        cXQv.wait_abs(s, mk[("xq2", pc - 2)])
                    cWQv.wait_abs(s, mk[("wi2", pc)])
                    sl = pc % 2
                    cXQs.inc(s.activation(out=itile[:, D * sl:D * (sl + 1)],
                                          in_=xn[:, 4096 * kc + D * q4:4096 * kc + D * (q4 + 1)],
                                          func=AF.Identity, bias=zero_col[:],
                                          scale=sxi[:, 0:1]))
                    mk[("xq1", pc)] = cXQs.n

                @blk.vector
                def _(v, pc=pc, kc=kc, q4=q4):
                    cXQs.wait_abs(v, mk[("xq1", pc)])
                    sl = pc % 2
                    cXQv.inc(v.tensor_scalar(out=xn[:, 4096 * kc + D * q4:4096 * kc + D * (q4 + 1)],
                                             in0=itile[:, D * sl:D * (sl + 1)],
                                             scalar1=sxi[:, 1:2], scalar2=None, op0=ALU.mult))
                    mk[("xq2", pc)] = cXQv.n
                    if pc == 31:
                        mk["xqall"] = cXQv.n

            for jc in range(32):
                @blk.sync
                def _(sync, jc=jc):
                    if jc == 0:
                        bDEv.wait(sync); bUPt.wait(sync); vH.wait(sync)
                        bKDv.wait(sync); bTRs.wait(sync); bMOs.wait(sync)
                        bMDv.wait(sync); bXPs.wait(sync); bC2v.wait(sync)
                        bSPs.wait(sync); bSTg.wait(sync); dSP.wait(sync)
                        bQKs.wait(sync); bVs.wait(sync); bKTs.wait(sync)
                        bN2s.wait(sync); bNs.wait(sync); bSQs.wait(sync)
                        bTTv.wait(sync); bCLg.wait(sync); bNSv.wait(sync)
                        gF.wait(sync); vC.wait(sync)
                    if jc >= 2:
                        cWQs.wait_abs(sync, mk[("wi1", jc - 2)])
                        cWQv.wait_abs(sync, mk[("wi2", jc - 2)])
                    dIr2[jc % 2].dinc(sync.dma_start(
                        out=wstg[:, D * (jc % 2):D * (jc % 2 + 1)],
                        in_=win_t[128 * jc:128 * (jc + 1), :]))
                    mk[("wld", jc)] = dIr2[jc % 2].n

                @blk.vector
                def _(v, jc=jc):
                    sl = jc % 2
                    dIr2[sl].wait_abs(v, mk[("wld", jc)])
                    cWQv.inc(v.tensor_reduce(out=wamax[:, sl:sl + 1],
                                             in_=wstg[:, D * sl:D * (sl + 1)], axis=AX,
                                             op=ALU.max, apply_absolute_value=True))
                    cWQv.wait(v)
                    cWQv.inc(v.tensor_scalar(out=wamax[:, sl:sl + 1], in0=wamax[:, sl:sl + 1],
                                             scalar1=1e-8, scalar2=None, op0=ALU.max))
                    cWQv.wait(v)
                    cWQv.inc(v.reciprocal(wamax[:, 4 + sl:5 + sl], wamax[:, sl:sl + 1]))
                    cWQv.wait(v)
                    cWQv.inc(v.tensor_scalar(out=wamax[:, 2 + sl:3 + sl],
                                             in0=wamax[:, 4 + sl:5 + sl], scalar1=127.0,
                                             scalar2=None, op0=ALU.mult))
                    cWQv.inc(v.tensor_scalar(out=wamax[:, 6 + sl:7 + sl],
                                             in0=wamax[:, sl:sl + 1], scalar1=1.0 / 127.0,
                                             scalar2=None, op0=ALU.mult))
                    mk[("wsc", jc)] = cWQv.n

                @blk.scalar
                def _(s, jc=jc):
                    sl = jc % 2
                    cWQv.wait_abs(s, mk[("wsc", jc)])
                    if jc >= 2:
                        cWQv.wait_abs(s, mk[("wi2", jc - 2)])
                        cXQv.wait_abs(s, mk[("xq2", jc - 2)])
                    cWQs.inc(s.activation(out=itile[:, D * sl:D * (sl + 1)],
                                          in_=wstg[:, D * sl:D * (sl + 1)],
                                          func=AF.Identity, bias=zero_col[:],
                                          scale=wamax[:, 2 + sl:3 + sl]))
                    mk[("wi1", jc)] = cWQs.n

                @blk.vector
                def _(v, jc=jc):
                    sl = jc % 2
                    cWQs.wait_abs(v, mk[("wi1", jc)])
                    if jc >= 2:
                        cWTt.wait_abs(v, mk[("wtt", jc - 2)])
                    cWQv.inc(v.tensor_scalar(out=wsimt[:, D * sl:D * (sl + 1)],
                                             in0=itile[:, D * sl:D * (sl + 1)],
                                             scalar1=wamax[:, 6 + sl:7 + sl], scalar2=None,
                                             op0=ALU.mult))
                    mk[("wi2", jc)] = cWQv.n

                @blk.tensor
                def _(t, jc=jc):
                    sl = jc % 2
                    cWQv.wait_abs(t, mk[("wi2", jc)])
                    if jc >= 2:
                        cWEs.wait_abs(t, mk[("wte", jc - 2)])
                    pb = psb[jc % 2]
                    last = None
                    for ib in range(8):
                        last = t.transpose(pb[:, 128 * ib:128 * (ib + 1)],
                                           wsimt[:, D * sl + 128 * ib:D * sl + 128 * (ib + 1)],
                                           eyeb_sb[:])
                    cWTt.inc(last)
                    mk[("wtt", jc)] = cWTt.n

                @blk.scalar
                def _(s, jc=jc):
                    cWTt.wait_abs(s, mk[("wtt", jc)])
                    pb = psb[jc % 2]
                    dst = win_sim.ap().rearrange("p (k a) -> p k a", a=DFF)[
                        :, :, 128 * jc:128 * (jc + 1)]
                    cWEs.inc(s.activation(out=dst,
                                          in_=pb.ap().rearrange("p (k a) -> p k a", a=128),
                                          func=AF.Identity, bias=zero_col[:], scale=1.0))
                    mk[("wte", jc)] = cWEs.n

                _c2piece(jc)

            for j in range(32):
                for tt in range(8):
                    g = 8 * j + tt
                    bk = ps[g % 6]

                    @blk.tensor
                    def _(t, j=j, tt=tt, g=g, bk=bk):
                        if g == 0:
                            cXQv.wait_abs(t, mk["xqall"])
                        if g >= 6:
                            cHs.wait_abs(t, mk[("hev", g - 6)])
                        kwait = mk.get(("wte", 31))
                        if g == 0:
                            cWEs.wait_abs(t, kwait)
                        for k in range(8):
                            ins = t.matmul(bk[:, 0:512],
                                           win_sim[:, DFF * k + 128 * j:DFF * k + 128 * (j + 1)],
                                           xn[:, 4096 * k + 512 * tt:4096 * k + 512 * (tt + 1)],
                                           start=(k == 0), stop=(k == 7))
                        cHt.inc(ins)
                        mk[("hmm", g)] = cHt.n

                    @blk.scalar
                    def _(s, j=j, tt=tt, g=g, bk=bk):
                        cHt.wait_abs(s, mk[("hmm", g)])
                        if tt == 0 and j >= 2:
                            cHD.wait_abs(s, mk[("hsp", j - 2)])
                            cAMv.wait_abs(s, mk[("ham", j - 2)])
                        sl = j % 2
                        cHs.inc(s.activation(out=hstg[:, T * sl + 512 * tt:T * sl + 512 * (tt + 1)],
                                             in_=bk[:, 0:512], func=AF.Gelu,
                                             bias=bin_sb[:, j:j + 1], scale=1.0))
                        mk[("hev", g)] = cHs.n

                @blk.vector
                def _(v, j=j):
                    cHs.wait_abs(v, mk[("hev", 8 * j + 7)])
                    sl = j % 2
                    cAMv.inc(v.tensor_reduce(out=hamax_cols[:, j:j + 1],
                                             in_=hstg[:, T * sl:T * (sl + 1)], axis=AX,
                                             op=ALU.max, apply_absolute_value=True))
                    mk[("ham", j)] = cAMv.n

                @blk.sync
                def _(sync, j=j):
                    cHs.wait_abs(sync, mk[("hev", 8 * j + 7)])
                    cHD.wait(sync)
                    sl = j % 2
                    cHD.dinc(sync.dma_start(out=h_spill[128 * j:128 * (j + 1), :],
                                            in_=hstg[:, T * sl:T * (sl + 1)]))
                    mk[("hsp", j)] = cHD.n

            @blk.vector
            def _(v):
                cAMv.wait_abs(v, mk[("ham", 31)])
                cAMv.inc(v.tensor_reduce(out=hamax_cols[:, 36:37], in_=hamax_cols[:, 0:32],
                                         axis=AX, op=ALU.max))

            @blk.tensor
            def _(t):
                cAMv.wait(t)
                cHs.wait_abs(t, mk[("hev", 255)])
                cAMt.inc(t.transpose(ps[0][0:1, 0:128], hamax_cols[:, 36:37], eyef_sb[:]))

            @blk.vector
            def _(v):
                cAMt.wait(v)
                cAMv.inc(v.tensor_reduce(out=amax_sc[0:1, 0:1], in_=ps[0][0:1, 0:128],
                                         axis=AX, op=ALU.max))

            @blk.gpsimd
            def _(g):
                cAMv.wait(g)
                dCC2.dinc(g.dma_start(out=cchi[0:1, :], in_=amax_sc[0:1, :]))
                dCC2.wait(g)
                cCC2.inc(g.collective_compute("AllReduce", ALU.max,
                                              replica_groups=[list(range(8))],
                                              ins=[cchi.opt()], outs=[ccho.opt()]))
                cCC2.wait(g)
                dCC2.dinc(g.dma_start(out=globh_sc[0:1, :], in_=ccho[0:1, :]))

            if "wis" in dbg_out:
                @blk.sync
                def _(sync):
                    cWEs.wait(sync)
                    dST.dinc(sync.dma_start(out=dbg_out["wis"][:, :], in_=win_sim[:]))
            if "amxh" in dbg_out:
                @blk.sync
                def _(sync):
                    dCC2.wait(sync)
                    dST.dinc(sync.dma_start(out=dbg_out["amxh"][0:1, :], in_=globh_sc[0:1, :]))

        if BSTEP >= 10:
            gC.close()
            gXN.close()

            gD = ExitStack()
            wo_sim = gD.enter_context(nc.sbuf_tensor("wo_sim", [128, 32 * D], BF16))
            wo_att = gD.enter_context(nc.sbuf_tensor("wo_att", [128, 8 * D], BF16))
            itile2 = gD.enter_context(nc.sbuf_tensor("itile2", [128, 2 * D], I32))
            hslab = gD.enter_context(nc.sbuf_tensor("hslab", [128, 2 * 32 * 256], BF16))
            ctxsl = gD.enter_context(nc.sbuf_tensor("ctxsl", [128, 2 * 8 * 256], BF16))
            resf = gD.enter_context(nc.sbuf_tensor("resf", [128, 2 * 8 * 256], F32))
            xre = gD.enter_context(nc.sbuf_tensor("xre", [128, 2 * 2 * D], F32))
            otm = gD.enter_context(nc.sbuf_tensor("otm", [128, 2 * 2 * D], F32))

            dWA = dW; eSHv = vM; eHQs = sD; eHQv = vD
            eOt = tO; eOv = vO; eRt = tR; eRv = vA

            @blk.tensor
            def _(t):
                dCC2.wait(t)
                cAMt.wait(t)
                tC.inc(t.matmul(ps[0][:, 0:1], ones_row[0:1, :], globh_sc[0:1, 0:1],
                                start=True, stop=True))
                mk["shbc"] = tC.n

            @blk.vector
            def _(v):
                tC.wait_abs(v, mk["shbc"])
                eSHv.inc(v.tensor_scalar(out=sxi[:, 6:7], in0=ps[0][:, 0:1],
                                         scalar1=1e-8, scalar2=None, op0=ALU.max))
                eSHv.wait(v)
                eSHv.inc(v.reciprocal(sxi[:, 7:8], sxi[:, 6:7]))
                eSHv.wait(v)
                eSHv.inc(v.tensor_scalar(out=sxi[:, 2:3], in0=sxi[:, 7:8], scalar1=32767.0,
                                         scalar2=None, op0=ALU.mult))
                eSHv.inc(v.tensor_scalar(out=sxi[:, 3:4], in0=sxi[:, 6:7],
                                         scalar1=1.0 / 32767.0, scalar2=None, op0=ALU.mult))
                mk["sh"] = eSHv.n

            @blk.gpsimd
            def _(g):
                cHt.wait(g); cHs.wait(g); cHD.wait(g); cAMv.wait(g)
                cWEs.wait(g); cXQv.wait(g); cXQs.wait(g); cWQv.wait(g); cWTt.wait(g)
                dST.wait(g)
                dWA.dinc(g.dma_start(out=wo_att.ap().rearrange("p (c n) -> p c n", n=D),
                                     in_=wout.rearrange("(c p) n -> p c n", p=128)))

            @blk.sync
            def _(sync):
                cHt.wait(sync); cHs.wait(sync); cHD.wait(sync)
                cWEs.wait(sync); cXQv.wait(sync); cAMv.wait(sync)
                dST.wait(sync)
                dK.dinc(sync.dma_start(
                    out=wo_sim.ap().rearrange("p (c n) -> p c n", n=D),
                    in_=wos_spill.rearrange("(c p) n -> p c n", p=128)))
                mk["wosld"] = dK.n

            TT = 256

            def emit_residual(rt):
                rhf = rt % 2
                for q2 in range(2):
                    @blk.tensor
                    def _(t, t2=rt, q2=q2):
                        if q2 == 0:
                            eOv.wait_abs(t, mk[("oev", t2, 7)])
                        for half in range(2):
                            bk = ps[4 + half]
                            if q2 == 1:
                                eRv.wait_abs(t, mk[("rtv", t2, 0)])
                            last = None
                            for oo in range(4):
                                o = 4 * half + oo
                                last = t.transpose(bk[:, 128 * oo:128 * (oo + 1)],
                                                   resf[:, 2048 * (t2 % 2) + TT * o + 128 * q2:
                                                        2048 * (t2 % 2) + TT * o + 128 * (q2 + 1)],
                                                   eyef_sb[:])
                            eRt.inc(last)
                            mk[("rtt", t2, q2, half)] = eRt.n

                    @blk.vector
                    def _(v, t2=rt, hf=rhf, q2=q2):
                        dRL.wait_abs(v, mk[("dld", t2)])
                        if t2 >= 2 and q2 == 0:
                            dST.wait_abs(v, mk[("ost", t2 - 2)])
                        for half in range(2):
                            bk = ps[4 + half]
                            eRt.wait_abs(v, mk[("rtt", t2, q2, half)])
                            eRv.inc(v.scalar_tensor_tensor(
                                out=otm[:, 2 * D * hf + D * q2 + 512 * half:2 * D * hf + D * q2 + 512 * (half + 1)],
                                in0=bk[:, 0:512], scalar=1.0,
                                in1=xre[:, 2 * D * hf + D * q2 + 512 * half:2 * D * hf + D * q2 + 512 * (half + 1)],
                                op0=ALU.mult, op1=ALU.add))
                        mk[("rtv", t2, q2)] = eRv.n
                        if q2 == 1:
                            mk[("rev", t2)] = eRv.n

                @blk.sync
                def _(sync, t2=rt, hf=rhf):
                    eRv.wait_abs(sync, mk[("rev", t2)])
                    dST.wait(sync)
                    dST.dinc(sync.dma_start(
                        out=out.rearrange("(a p) c -> p a c", p=128)[:, 2 * t2:2 * (t2 + 1), :],
                        in_=otm.ap().rearrange("p (b a c) -> p b a c", b=2, c=D)[:, hf]))
                    mk[("ost", t2)] = dST.n

            for t2 in range(16):
                hf = t2 % 2

                @blk.sync
                def _(sync, t2=t2, hf=hf):
                    if t2 >= 1:
                        dRL.wait_abs(sync, mk[("dld", t2 - 1)])
                    if t2 >= 2:
                        eOt.wait_abs(sync, mk[("omm", t2 - 2, 7)])
                        eRv.wait_abs(sync, mk[("rev", t2 - 2)])
                    dRL.dinc(sync.dma_start(
                        out=hslab.ap().rearrange("p (b j w) -> p b j w", b=2, w=TT)[:, hf],
                        in_=h_spill.rearrange("(j p) t -> p j t", p=128)[:, :, TT * t2:TT * (t2 + 1)]))
                    dRL.dinc(sync.dma_start(
                        out=ctxsl.ap().rearrange("p (b c w) -> p b c w", b=2, w=TT)[:, hf],
                        in_=ctx_spill.rearrange("(c p) t -> p c t", p=128)[:, :, TT * t2:TT * (t2 + 1)]))
                    dRL.dinc(sync.dma_start(
                        out=xre.ap().rearrange("p (b a c) -> p b a c", b=2, c=D)[:, hf],
                        in_=x.rearrange("(a p) c -> p a c", p=128)[:, 2 * t2:2 * (t2 + 1), :]))
                    mk[("dld", t2)] = dRL.n

                for pc in range(8):
                    @blk.scalar
                    def _(s, t2=t2, hf=hf, pc=pc):
                        if pc == 0:
                            dRL.wait_abs(s, mk[("dld", t2)])
                            eSHv.wait_abs(s, mk["sh"])
                        if pc < 2 and t2 >= 1:
                            eHQv.wait_abs(s, mk[("hq2", t2 - 1, 6 + pc)])
                        if pc >= 2:
                            eHQv.wait_abs(s, mk[("hq2", t2, pc - 2)])
                        sl = pc % 2
                        eHQs.inc(s.activation(out=itile2[:, D * sl:D * (sl + 1)],
                                              in_=hslab[:, 8192 * hf + 1024 * pc:8192 * hf + 1024 * (pc + 1)],
                                              func=AF.Identity, bias=zero_col[:],
                                              scale=sxi[:, 2:3]))
                        mk[("hq1", t2, pc)] = eHQs.n

                    @blk.vector
                    def _(v, t2=t2, hf=hf, pc=pc):
                        eHQs.wait_abs(v, mk[("hq1", t2, pc)])
                        sl = pc % 2
                        eHQv.inc(v.tensor_scalar(out=hslab[:, 8192 * hf + 1024 * pc:8192 * hf + 1024 * (pc + 1)],
                                                 in0=itile2[:, D * sl:D * (sl + 1)],
                                                 scalar1=sxi[:, 3:4], scalar2=None,
                                                 op0=ALU.mult))
                        mk[("hq2", t2, pc)] = eHQv.n

                if t2 >= 1:
                    emit_residual(t2 - 1)

                for o in range(8):
                    bk = ps[o % 6]

                    @blk.tensor
                    def _(t, t2=t2, hf=hf, o=o, bk=bk):
                        if t2 == 0 and o == 0:
                            dK.wait_abs(t, mk["wosld"])
                            dWA.wait(t)
                            eSHv.wait_abs(t, mk["sh"])
                        go = 8 * t2 + o
                        if go >= 6:
                            pgo = go - 6
                            eOv.wait_abs(t, mk[("oev", pgo // 8, pgo % 8)])
                        if o in (4, 5) and t2 >= 1:
                            eRv.wait_abs(t, mk[("rtv", t2 - 1, 1)])
                        eHQv.wait_abs(t, mk[("hq2", t2, 7)])
                        dRL.wait_abs(t, mk[("dld", t2)])
                        for j in range(32):
                            t.matmul(bk[:, 0:TT],
                                     wo_sim[:, D * j + 128 * o:D * j + 128 * (o + 1)],
                                     hslab[:, 8192 * hf + TT * j:8192 * hf + TT * (j + 1)],
                                     start=(j == 0), stop=False)
                        for k in range(8):
                            ins = t.matmul(bk[:, 0:TT],
                                           wo_att[:, D * k + 128 * o:D * k + 128 * (o + 1)],
                                           ctxsl[:, 2048 * hf + TT * k:2048 * hf + TT * (k + 1)],
                                           start=False, stop=(k == 7))
                        eOt.inc(ins)
                        mk[("omm", t2, o)] = eOt.n

                    @blk.vector
                    def _(v, t2=t2, o=o, bk=bk):
                        eOt.wait_abs(v, mk[("omm", t2, o)])
                        if t2 >= 2:
                            eRt.wait_abs(v, mk[("rtt", t2 - 2, 1, 1)])
                        eOv.inc(v.tensor_scalar(out=resf[:, 2048 * (t2 % 2) + TT * o:2048 * (t2 % 2) + TT * (o + 1)],
                                                in0=bk[:, 0:TT], scalar1=bcomb[:, o:o + 1],
                                                scalar2=None, op0=ALU.add))
                        mk[("oev", t2, o)] = eOv.n

            emit_residual(15)

            if "wos" in dbg_out:
                @blk.sync
                def _(sync):
                    dK.wait_abs(sync, mk["wosld"])
                    dST.dinc(sync.dma_start(out=dbg_out["wos"][:, :], in_=wo_sim[:]))
            gD.close()

        if "xn" in dbg_out:
            @blk.sync
            def _(sync):
                sA.wait(sync)
                dST.dinc(sync.dma_start(out=dbg_out["xn"][:, :], in_=xn[:]))

        if "amx" in dbg_out:
            @blk.sync
            def _(sync):
                dCC.wait(sync)
                dST.dinc(sync.dma_start(out=dbg_out["amx"][0:1, :], in_=glob_sc[0:1, :]))

        if BSTEP < 9:
            gB.close()
            gBc.close()
        if 9 <= BSTEP < 10:
            gC.close()
        if BSTEP < 10:
            gXN.close()

        @blk.sync
        def _(sync):
            dST.wait(sync)
            dCC.wait(sync)
            vA.wait(sync)
            sA.wait(sync)

    return nc, dbg_out


def _prep_inputs(L0_attn_x, L0_attn_Wqkv, L0_attn_bqkv, L0_attn_Wout, L0_attn_bout,
                 L0_attn_omega, L0_attn_quad_nodes, L0_attn_quad_weights,
                 L0_ffn_W_in, L0_ffn_b_in, L0_ffn_W_out, L0_ffn_b_out):
    consts = host_consts()
    omega_t = np.ascontiguousarray(
        np.asarray(L0_attn_omega, dtype=np.float32).transpose(1, 2, 0, 3).reshape(H * HD, FD))
    win_t = np.ascontiguousarray(np.asarray(L0_ffn_W_in, dtype=np.float32).T)
    base = {
        "wqkv": np.ascontiguousarray(np.asarray(L0_attn_Wqkv, dtype=np.float32)),
        "bqkv": np.ascontiguousarray(np.asarray(L0_attn_bqkv, dtype=np.float32)),
        "wout": np.ascontiguousarray(np.asarray(L0_attn_Wout, dtype=np.float32)),
        "bout": np.ascontiguousarray(np.asarray(L0_attn_bout, dtype=np.float32)),
        "omega_t": omega_t,
        "qn": np.ascontiguousarray(np.asarray(L0_attn_quad_nodes, dtype=np.float32).reshape(2, 1)),
        "qw": np.ascontiguousarray(np.asarray(L0_attn_quad_weights, dtype=np.float32).reshape(2, 1)),
        "win_t": win_t,
        "b_in": np.ascontiguousarray(np.asarray(L0_ffn_b_in, dtype=np.float32)),
        "wofn": np.ascontiguousarray(np.asarray(L0_ffn_W_out, dtype=np.float32)),
        "b_out": np.ascontiguousarray(np.asarray(L0_ffn_b_out, dtype=np.float32)),
        "eye_f": consts["eye_f"], "eye_b": consts["eye_b"], "tri": consts["tri"],
        "m16": consts["m16"], "ihp": consts["ihp"], "sel2": consts["sel2"],
    }
    xs = np.asarray(L0_attn_x, dtype=np.float32)
    return [dict(base, x=np.ascontiguousarray(xs[b])) for b in range(8)]


def kernel(L0_attn_x, L0_attn_Wqkv, L0_attn_bqkv, L0_attn_Wout, L0_attn_bout,
           L0_attn_omega, L0_attn_quad_nodes, L0_attn_quad_weights,
           L0_ffn_W_in, L0_ffn_b_in, L0_ffn_W_out, L0_ffn_b_out,
           _dbg=(), _phases=4):
    in_maps = _prep_inputs(L0_attn_x, L0_attn_Wqkv, L0_attn_bqkv, L0_attn_Wout,
                           L0_attn_bout, L0_attn_omega, L0_attn_quad_nodes,
                           L0_attn_quad_weights, L0_ffn_W_in, L0_ffn_b_in,
                           L0_ffn_W_out, L0_ffn_b_out)
    nc, dbg_out = build_nc(dbg=_dbg, phases=_phases)
    res = run_bass_kernel_spmd(nc, in_maps, core_ids=list(range(8)))
    outs = np.stack([np.asarray(res.results[b]["out"]) for b in range(8)], axis=0)
    if _dbg:
        return outs, [res.results[b] for b in range(8)]
    return outs


# version 12 (final)
# speedup vs baseline: 1.0342x; 1.0342x over previous
import numpy as np
import ml_dtypes
from contextlib import ExitStack

import concourse.bass as bass
import concourse.mybir as mybir
from concourse.bass_utils import run_bass_kernel_spmd

F32 = mybir.dt.float32
BF16 = mybir.dt.bfloat16
FP16 = mybir.dt.float16
I32 = mybir.dt.int32
AF = mybir.ActivationFunctionType
ALU = mybir.AluOpType
AX = mybir.AxisListType.X

BSTEP = 99
NSCRUN = NSC_DEFAULT = 8
S12SUB = 9
S9SUB = 9
T = 4096
D = 1024
H = 16
HD = 64
FD = 64
DFF = 4096
SC = 512
NSC = T // SC
NTT = T // 128


def load_act_set(sc_eng, set_id):
    inst = mybir.InstLoadActFuncSet(act_func_set_id=set_id,
                                    name=f"actload_{set_id}_{load_act_set.k}",
                                    engine=mybir.EngineType.Activation)
    load_act_set.k += 1
    sc_eng.add_instruction(inst)


load_act_set.k = 0


class Ctr:

    def __init__(self, stack, nc, name):
        self.h = stack.enter_context(nc.semaphore(name))
        self.n = 0

    def inc(self, instr, k=1):
        instr.then_inc(self.h, k)
        self.n += k
        return instr

    def dinc(self, instr):
        return self.inc(instr, 16)

    def wait(self, eng, lag=0):
        eng.wait_ge(self.h, max(self.n - lag, 0))

    def wait_abs(self, eng, val):
        eng.wait_ge(self.h, val)


def host_consts():
    eye_f = np.eye(128, dtype=np.float32)
    eye_b = np.eye(128, dtype=ml_dtypes.bfloat16)
    tri = np.triu(np.ones((128, 128), dtype=np.float32)).astype(ml_dtypes.bfloat16)
    m16 = np.zeros((128, 8 * 16), dtype=np.float32)
    for c in range(8):
        m16[0:64, 16 * c + 2 * c] = 1.0
        m16[64:128, 16 * c + 2 * c + 1] = 1.0
    m16 = m16.astype(ml_dtypes.bfloat16)
    ihp = np.zeros((16, 8 * 128), dtype=np.float32)
    for hp in range(8):
        ihp[2 * hp, 128 * hp: 128 * hp + 64] = 1.0
        ihp[2 * hp + 1, 128 * hp + 64: 128 * hp + 128] = 1.0
    ihp = ihp.astype(np.float16)
    sel2 = np.zeros((2, 128), dtype=np.float32)
    p = np.arange(128)
    sel2[0, (p % 64) // 32 == 0] = 1.0
    sel2[1, (p % 64) // 32 == 1] = 1.0
    return dict(eye_f=eye_f, eye_b=eye_b, tri=tri, m16=m16, ihp=ihp, sel2=sel2)


def build_nc(dbg=(), phases=4):
    nc = bass.Bass("TRN2", target_bir_lowering=False, debug=False)
    dbg = set(dbg)

    x = nc.dram_tensor("x", [T, D], F32, kind="ExternalInput").ap()
    wqkv = nc.dram_tensor("wqkv", [D, 3 * D], F32, kind="ExternalInput").ap()
    bqkv = nc.dram_tensor("bqkv", [3 * D], F32, kind="ExternalInput").ap()
    wout = nc.dram_tensor("wout", [D, D], F32, kind="ExternalInput").ap()
    bout = nc.dram_tensor("bout", [D], F32, kind="ExternalInput").ap()
    omega_t = nc.dram_tensor("omega_t", [H * HD, FD], F32, kind="ExternalInput").ap()
    qn = nc.dram_tensor("qn", [2, 1], F32, kind="ExternalInput").ap()
    qw = nc.dram_tensor("qw", [2, 1], F32, kind="ExternalInput").ap()
    win_t = nc.dram_tensor("win_t", [DFF, D], F32, kind="ExternalInput").ap()
    b_in = nc.dram_tensor("b_in", [DFF], F32, kind="ExternalInput").ap()
    wofn = nc.dram_tensor("wofn", [DFF, D], F32, kind="ExternalInput").ap()
    b_out = nc.dram_tensor("b_out", [D], F32, kind="ExternalInput").ap()
    eye_f = nc.dram_tensor("eye_f", [128, 128], F32, kind="ExternalInput").ap()
    eye_b = nc.dram_tensor("eye_b", [128, 128], BF16, kind="ExternalInput").ap()
    tri = nc.dram_tensor("tri", [128, 128], BF16, kind="ExternalInput").ap()
    m16 = nc.dram_tensor("m16", [128, 128], BF16, kind="ExternalInput").ap()
    ihp = nc.dram_tensor("ihp", [16, 1024], FP16, kind="ExternalInput").ap()
    sel2 = nc.dram_tensor("sel2", [2, 128], F32, kind="ExternalInput").ap()

    out = nc.dram_tensor("out", [T, D], F32, kind="ExternalOutput").ap()

    h_spill = nc.dram_tensor("h_spill", [DFF, T], BF16).ap()
    ctx_spill = nc.dram_tensor("ctx_spill", [D, T], BF16).ap()
    wos_spill = nc.dram_tensor("wos_spill", [DFF, D], BF16).ap()
    ccxi = nc.dram_tensor("ccxi", [1, 8], F32).ap()
    ccxo = nc.dram_tensor("ccxo", [1, 8], F32).ap()
    cchi = nc.dram_tensor("cchi", [1, 8], F32).ap()
    ccho = nc.dram_tensor("ccho", [1, 8], F32).ap()

    dbg_out = {}

    def dbg_tensor(name, shape, dt):
        if name in dbg:
            dbg_out[name] = nc.dram_tensor("dbg_" + name, shape, dt, kind="ExternalOutput").ap()

    dbg_tensor("xn", [128, NTT * 8 * 128], BF16)
    dbg_tensor("qkfm", [128, 16 * SC], BF16)
    dbg_tensor("qf", [128, 8 * SC], BF16)
    dbg_tensor("kf", [128, 8 * SC], BF16)
    dbg_tensor("vplus", [128, 4 * 1040], BF16)
    dbg_tensor("ctx", [128, 4 * 1024], BF16)
    dbg_tensor("ctx1", [128, 4 * 1024], BF16)
    dbg_tensor("amx", [1, 8], F32)
    dbg_tensor("amxh", [1, 8], F32)
    dbg_tensor("hrow", [128, T], BF16)
    dbg_tensor("wis", [128, 8 * DFF], BF16)
    dbg_tensor("wos", [128, 32 * D], BF16)

    with ExitStack() as st_:
        e = st_.enter_context

        def C(name):
            return Ctr(st_, nc, name)

        dXr = [C("dXr0"), C("dXr1"), C("dXr2"), C("dXr3"), C("dXr4"), C("dXr5")]
        dK = C("dK");  dW = C("dW");  dWr2 = [C("dWr0"), C("dWr1")]; dWS = C("dWS")
        dOr2 = [C("dOr0"), C("dOr1")]; dIr2 = [C("dIr0"), C("dIr1")]
        dSP = C("dSP"); dRL = C("dRL"); dST = C("dST"); dCC = C("dCC"); dCC2 = C("dCC2")
        cCC = C("cCC"); cCC2 = C("cCC2")
        tC = C("tC");  tA = C("tA");  tQ = C("tQ");  tF = C("tF");  tT = C("tT")
        tAT = C("tAT"); tCX = C("tCX"); tUP = C("tUP"); tTR = C("tTR")
        tW = C("tW");  tH = C("tH");  tO = C("tO");  tR = C("tR")
        vC = C("vC");  vA = C("vA");  vF = C("vF");  vM = C("vM");  vX = C("vX")
        vQ = C("vQ");  vW = C("vW");  vH = C("vH");  vD = C("vD");  vO = C("vO")
        sC = C("sC");  sA = C("sA");  sQ = C("sQ");  sF = C("sF");  sS = C("sS")
        sW = C("sW");  sH = C("sH");  sD = C("sD")
        gF = C("gF")


        eyef_sb = e(nc.sbuf_tensor("eyef_sb", [128, 128], F32))
        zero_col = e(nc.sbuf_tensor("zero_col", [128, 1], F32))
        eps_col = e(nc.sbuf_tensor("eps_col", [128, 1], F32))
        ones_row = e(nc.sbuf_tensor("ones_row", [1, 128], F32))
        s_col = e(nc.sbuf_tensor("s_col", [128, 1], F32))
        negs_col = e(nc.sbuf_tensor("negs_col", [128, 1], F32))
        sq2_col = e(nc.sbuf_tensor("sq2_col", [128, 1], F32))
        c2_col = e(nc.sbuf_tensor("c2_col", [128, 1], F32))
        lnc2_col = e(nc.sbuf_tensor("lnc2_col", [128, 1], F32))
        bin_sb = e(nc.sbuf_tensor("bin_sb", [128, 32], F32))
        bcomb = e(nc.sbuf_tensor("bcomb", [128, 8], F32))
        bo1_sb = e(nc.sbuf_tensor("bo1_sb", [128, 8], F32))
        bo2_sb = e(nc.sbuf_tensor("bo2_sb", [128, 8], F32))
        amax_sc = e(nc.sbuf_tensor("amax_sc", [1, 8], F32))
        glob_sc = e(nc.sbuf_tensor("glob_sc", [1, 8], F32))
        globh_sc = e(nc.sbuf_tensor("globh_sc", [1, 8], F32))
        sxi = e(nc.sbuf_tensor("sxi", [128, 8], F32))
        eyeb_sb = e(nc.sbuf_tensor("eyeb_sb", [128, 128], BF16))
        ps = [e(nc.psum_tensor(f"ps{i}", [128, 512], F32)) for i in range(6)]
        psb = [e(nc.psum_tensor(f"psb{i}", [128, 1024], BF16)) for i in range(2)]

        gXN = ExitStack()
        xn = gXN.enter_context(nc.sbuf_tensor("xn", [128, NTT * 8 * 128], BF16))
        amax_cols = gXN.enter_context(nc.sbuf_tensor("amax_cols", [128, 40], F32))

        gBc = ExitStack()
        wq_sb = gBc.enter_context(nc.sbuf_tensor("wq_sb", [128, 8 * 3 * D], BF16))
        omg = gBc.enter_context(nc.sbuf_tensor("omg", [128, 8 * FD], BF16))
        tri_sb = gBc.enter_context(nc.sbuf_tensor("tri_sb", [128, 128], BF16))
        m16_sb = gBc.enter_context(nc.sbuf_tensor("m16_sb", [128, 128], BF16))
        ihpq_sb = gBc.enter_context(nc.sbuf_tensor("ihpq_sb", [16, 1024], FP16))
        bq_sb = gBc.enter_context(nc.sbuf_tensor("bq_sb", [128, 16], F32))
        BV = gBc.enter_context(nc.sbuf_tensor("BV", [128, D], F32))

        gP0 = ExitStack()
        ihp_sb = gP0.enter_context(nc.sbuf_tensor("ihp_sb", [16, 1024], FP16))
        sel2_sb = gP0.enter_context(nc.sbuf_tensor("sel2_sb", [2, 128], F32))
        qn_sb = gP0.enter_context(nc.sbuf_tensor("qn_sb", [2, 1], F32))
        qw_sb = gP0.enter_context(nc.sbuf_tensor("qw_sb", [2, 1], F32))
        bv_row = gP0.enter_context(nc.sbuf_tensor("bv_row", [1, D], F32))
        sq2_row = gP0.enter_context(nc.sbuf_tensor("sq2_row", [1, 128], F32))
        sq16 = gP0.enter_context(nc.sbuf_tensor("sq16", [16, 128], F32))

        blk = e(nc.Block())

        mk = {}
        MOFF = [0, 512, 896, 1152]

        @blk.sync
        def _(sync):
            for dst, src in [(eyef_sb, eye_f), (eyeb_sb, eye_b), (tri_sb, tri),
                             (m16_sb, m16), (ihp_sb, ihp), (sel2_sb, sel2),
                             (qn_sb, qn), (qw_sb, qw)]:
                dK.dinc(sync.dma_start(out=dst[:], in_=src[:, :]))
            dK.dinc(sync.dma_start(out=bq_sb[:],
                                   in_=bqkv[0:2048].rearrange("(c p) -> p c", p=128),
                                   allow_slow_non_contiguous=True))
            dK.dinc(sync.dma_start(out=bv_row[:],
                                   in_=bqkv[2048:3072].rearrange("(a c) -> a c", a=1)))
            dK.dinc(sync.dma_start(out=bin_sb[:],
                                   in_=b_in.rearrange("(c p) -> p c", p=128),
                                   allow_slow_non_contiguous=True))
            dK.dinc(sync.dma_start(out=bo1_sb[:],
                                   in_=bout.rearrange("(c p) -> p c", p=128),
                                   allow_slow_non_contiguous=True))
            dK.dinc(sync.dma_start(out=bo2_sb[:],
                                   in_=b_out.rearrange("(c p) -> p c", p=128),
                                   allow_slow_non_contiguous=True))

        @blk.gpsimd
        def _(g):
            dW.dinc(g.dma_start(out=omg.ap().rearrange("p (r m) -> p r m", m=FD),
                                in_=omega_t.rearrange("(r u d) m -> (u d) r m", u=2, d=64)))

        @blk.vector
        def _(v):
            vC.inc(v.memset(zero_col[:], 0.0))
            vC.inc(v.memset(eps_col[:], 1e-5))
            vC.inc(v.memset(ones_row[:], 1.0))
            vC.inc(v.memset(amax_sc[:], 0.0))

        @blk.tensor
        def _(t):
            dK.wait(t)
            t.matmul(ps[0][:, 0:1], sel2_sb[:, :], qn_sb[:, :], start=True, stop=True)
            tC.inc(t.matmul(ps[0][:, 4:5], sel2_sb[:, :], qw_sb[:, :], start=True, stop=True))

        @blk.vector
        def _(v):
            tC.wait(v)
            vC.inc(v.tensor_copy(s_col[:], ps[0][:, 0:1]))
            vC.inc(v.tensor_scalar(out=negs_col[:], in0=ps[0][:, 0:1], scalar1=-1.0,
                                   scalar2=None, op0=ALU.mult))
            vC.inc(v.tensor_scalar(out=sxi[:, 7:8], in0=ps[0][:, 0:1], scalar1=0.0,
                                   scalar2=2.0, op0=ALU.max, op1=ALU.mult))
            vC.inc(v.tensor_scalar(out=c2_col[:], in0=ps[0][:, 4:5], scalar1=0.0,
                                   scalar2=1.0 / 32.0, op0=ALU.max, op1=ALU.mult))

        @blk.scalar
        def _(s):
            vC.wait(s)
            load_act_set(s, 3)
            sC.inc(s.activation(out=sq2_col[:], in_=sxi[:, 7:8], func=AF.Sqrt,
                                bias=zero_col[:], scale=1.0))
            load_act_set(s, 6)
            sC.inc(s.activation(out=lnc2_col[:], in_=c2_col[:], func=AF.Ln,
                                bias=zero_col[:], scale=1.0))

        @blk.tensor
        def _(t):
            sC.wait(t)
            tC.inc(t.transpose(ps[1][0:1, 0:128], sq2_col[:, 0:1], eyef_sb[:]))

        @blk.vector
        def _(v):
            tC.wait(v)
            vC.inc(v.tensor_copy(sq2_row[:], ps[1][0:1, 0:128]))

        @blk.tensor
        def _(t):
            vC.wait(t)
            t.matmul(ps[1][0:16, 0:128], ones_row[0:1, 0:16], sq2_row[0:1, :],
                     start=True, stop=True)
            t.matmul(ps[2][:, 0:512], ones_row[0:1, :], bv_row[0:1, 0:512],
                     start=True, stop=True)
            tC.inc(t.matmul(ps[3][:, 0:512], ones_row[0:1, :], bv_row[0:1, 512:1024],
                            start=True, stop=True))

        @blk.vector
        def _(v):
            tC.wait(v)
            vC.inc(v.tensor_copy(sq16[:], ps[1][0:16, 0:128]))
            vC.inc(v.tensor_copy(BV[:, 0:512], ps[2][:, 0:512]))
            vC.inc(v.tensor_copy(BV[:, 512:1024], ps[3][:, 0:512]))
            vC.wait(v)
            for hp in range(8):
                vC.inc(v.tensor_tensor(out=ihpq_sb[:, 128 * hp:128 * (hp + 1)],
                                       in0=ihp_sb[:, 128 * hp:128 * (hp + 1)],
                                       in1=sq16[:, :], op=ALU.mult))
            vC.inc(v.tensor_tensor(out=bcomb[:], in0=bo1_sb[:], in1=bo2_sb[:], op=ALU.add))

        gP0.close()

        gA = ExitStack()
        wqstg = gA.enter_context(nc.sbuf_tensor("wqstg", [128, 2 * 3 * D], F32))
        wstgA = gA.enter_context(nc.sbuf_tensor("wstgA", [128, 2 * D], F32))
        wamaxA = gA.enter_context(nc.sbuf_tensor("wamaxA", [128, 8], F32))
        itileA = gA.enter_context(nc.sbuf_tensor("itileA", [128, 2 * D], I32))
        wsimA = gA.enter_context(nc.sbuf_tensor("wsimA", [128, 2 * D], BF16))

        for k8 in range(8):
            @blk.gpsimd
            def _(g, k8=k8):
                if k8 == 0:
                    tC.wait(g)
                    vC.wait(g)
                    sC.wait(g)
                if k8 >= 2:
                    vC.wait_abs(g, mk[("wqc", k8 - 2)])
                dWr2[k8 % 2].dinc(g.dma_start(
                    out=wqstg[:, 3 * D * (k8 % 2):3 * D * (k8 % 2 + 1)],
                    in_=wqkv[128 * k8:128 * (k8 + 1), :]))
                mk[("wql", k8)] = dWr2[k8 % 2].n

            @blk.vector
            def _(v, k8=k8):
                dWr2[k8 % 2].wait_abs(v, mk[("wql", k8)])
                vC.inc(v.tensor_copy(wq_sb[:, 3 * D * k8:3 * D * (k8 + 1)],
                                     wqstg[:, 3 * D * (k8 % 2):3 * D * (k8 % 2 + 1)]))
                mk[("wqc", k8)] = vC.n


        xt = gA.enter_context(nc.sbuf_tensor("xt", [128, 3 * D], F32))
        bn6 = gA.enter_context(nc.sbuf_tensor("bn6", [128, 24], F32))
        mv2 = gA.enter_context(nc.sbuf_tensor("mv2", [128, 4], F32))
        rstd = gA.enter_context(nc.sbuf_tensor("rstd", [128, 4], F32))
        xn_tm = gA.enter_context(nc.sbuf_tensor("xn_tm", [128, 2 * D], BF16))

        ts_done = {}
        for i in range(NTT):
            sl = i % 3
            nsl = i % 2

            def _xload(eng, j):
                if j <= 1:
                    tC.wait(eng)
                    vC.wait(eng)
                    sC.wait(eng)
                if j >= 3:
                    vA.wait_abs(eng, ts_done[j - 3])
                dXr[j % 6].dinc(eng.dma_start(
                    out=xt[:, D * (j % 3):D * (j % 3 + 1)],
                    in_=x[128 * j:128 * (j + 1), :]))

            if i == 0:
                @blk.sync
                def _(sync):
                    _xload(sync, 0)

                @blk.scalar
                def _(s):
                    _xload(s, 1)

                @blk.sync
                def _(sync):
                    _xload(sync, 2)
            elif i + 2 < NTT:
                if (i + 2) % 2 == 0:
                    @blk.sync
                    def _(sync, i=i):
                        _xload(sync, i + 2)
                else:
                    @blk.scalar
                    def _(s, i=i):
                        _xload(s, i + 2)

            def _bn(v, j):
                dXr[j % 6].wait(v)
                if j >= 2:
                    sA.wait_abs(v, mk[("lnex", j - 2)])
                xs = xt[:, D * (j % 3):D * (j % 3 + 1)]
                par = j % 2
                vA.inc(v.bn_stats(bn6[:, 12 * par:12 * par + 6], xs[:, 0:512]))
                vA.inc(v.bn_stats(bn6[:, 12 * par + 6:12 * par + 12], xs[:, 512:1024]))
                vA.wait(v)
                vA.inc(v.bn_aggr(mv2[:, 2 * par:2 * (par + 1)], bn6[:, 12 * par:12 * (par + 1)]))
                mk[("aggr", j)] = vA.n

            if i == 0:
                @blk.vector
                def _(v):
                    _bn(v, 0)

            @blk.scalar
            def _(s, i=i):
                vA.wait_abs(s, mk[("aggr", i)])
                par = i % 2
                sA.inc(s.activation(out=rstd[:, 2 * par:2 * par + 1],
                                    in_=mv2[:, 2 * par + 1:2 * par + 2], func=AF.Ln,
                                    bias=eps_col[:], scale=1.0))
                sA.wait(s)
                sA.inc(s.activation(out=rstd[:, 2 * par + 1:2 * par + 2],
                                    in_=rstd[:, 2 * par:2 * par + 1], func=AF.Exp,
                                    bias=zero_col[:], scale=-0.5))
                mk[("lnex", i)] = sA.n

            if i + 1 < NTT:
                @blk.vector
                def _(v, i=i):
                    _bn(v, i + 1)

            @blk.vector
            def _(v, i=i, sl=sl, nsl=nsl):
                sA.wait_abs(v, mk[("lnex", i)])
                if i >= 2:
                    tA.wait_abs(v, i - 1)
                par = i % 2
                xs = xt[:, D * sl:D * (sl + 1)]
                xn_t = xn_tm[:, D * nsl:D * (nsl + 1)]
                vA.inc(v.tensor_scalar(out=xn_t, in0=xs, scalar1=mv2[:, 2 * par:2 * par + 1],
                                       scalar2=rstd[:, 2 * par + 1:2 * par + 2],
                                       op0=ALU.subtract, op1=ALU.mult))
                ts_done[i] = vA.n
                vA.wait(v)
                vA.inc(v.tensor_reduce(out=amax_cols[:, i:i + 1], in_=xn_t, axis=AX,
                                       op=ALU.max, apply_absolute_value=True))

            @blk.tensor
            def _(t, i=i, nsl=nsl):
                vA.wait(t)
                if i == 0:
                    vC.wait(t)
                if i >= 2:
                    sA.wait_abs(t, mk[("xnev", i - 2)])
                xn_t = xn_tm[:, D * nsl:D * (nsl + 1)]
                pb = psb[i % 2]
                last = None
                for c in range(8):
                    last = t.transpose(pb[:, 128 * c:128 * (c + 1)],
                                       xn_t[:, 128 * c:128 * (c + 1)], eyeb_sb[:])
                tA.inc(last)

            @blk.scalar
            def _(s, i=i):
                tA.wait(s)
                pb = psb[i % 2]
                dst = xn.ap().rearrange("p (c a) -> p c a", a=NTT * 128)[
                    :, :, 128 * i:128 * (i + 1)]
                sA.inc(s.activation(out=dst,
                                    in_=pb.ap().rearrange("p (c a) -> p c a", a=128),
                                    func=AF.Identity, bias=zero_col[:], scale=1.0))
                mk[("xnev", i)] = sA.n

            @blk.gpsimd
            def _(g, jc=i):
                if jc >= 2:
                    sW.wait_abs(g, mk[("oi2", jc - 2)])
                dOr2[jc % 2].dinc(g.dma_start(
                    out=wstgA[:, D * (jc % 2):D * (jc % 2 + 1)],
                    in_=wofn[128 * jc:128 * (jc + 1), :]))
                mk[("old", jc)] = dOr2[jc % 2].n

            @blk.vector
            def _(v, jc=i):
                sl = jc % 2
                dOr2[sl].wait_abs(v, mk[("old", jc)])
                vW.inc(v.tensor_reduce(out=wamaxA[:, sl:sl + 1],
                                       in_=wstgA[:, D * sl:D * (sl + 1)], axis=AX,
                                       op=ALU.max, apply_absolute_value=True))
                vW.wait(v)
                vW.inc(v.tensor_scalar(out=wamaxA[:, 6 + sl:7 + sl],
                                       in0=wamaxA[:, sl:sl + 1], scalar1=1e-8,
                                       scalar2=1.0 / 127.0, op0=ALU.max, op1=ALU.mult))
                vW.wait(v)
                vW.inc(v.reciprocal(wamaxA[:, 2 + sl:3 + sl], wamaxA[:, 6 + sl:7 + sl]))
                mk[("osc", jc)] = vW.n

            @blk.scalar
            def _(s, jc=i):
                sl = jc % 2
                vW.wait_abs(s, mk[("osc", jc)])
                if jc >= 2:
                    sW.wait_abs(s, mk[("oi2", jc - 2)])
                sW.inc(s.activation(out=itileA[:, D * sl:D * (sl + 1)],
                                    in_=wstgA[:, D * sl:D * (sl + 1)],
                                    func=AF.Identity, bias=zero_col[:],
                                    scale=wamaxA[:, 2 + sl:3 + sl]))
                mk[("oi1", jc)] = sW.n
                sW.wait(s)
                if jc >= 2:
                    dWS.wait_abs(s, mk[("osp", jc - 2)])
                sW.inc(s.activation(out=wsimA[:, D * sl:D * (sl + 1)],
                                    in_=itileA[:, D * sl:D * (sl + 1)],
                                    func=AF.Identity, bias=zero_col[:],
                                    scale=wamaxA[:, 6 + sl:7 + sl]))
                mk[("oi2", jc)] = sW.n

            @blk.gpsimd
            def _(g, jc=i):
                sl = jc % 2
                sW.wait_abs(g, mk[("oi2", jc)])
                dWS.wait(g)
                dWS.dinc(g.dma_start(out=wos_spill[128 * jc:128 * (jc + 1), :],
                                     in_=wsimA[:, D * sl:D * (sl + 1)]))
                mk[("osp", jc)] = dWS.n

        @blk.vector
        def _(v):
            vA.wait(v)
            vA.inc(v.tensor_reduce(out=amax_cols[:, 36:37], in_=amax_cols[:, 0:32],
                                   axis=AX, op=ALU.max))

        @blk.tensor
        def _(t):
            vA.wait(t)
            tA.inc(t.transpose(ps[5][0:1, 256:384], amax_cols[:, 36:37], eyef_sb[:]))

        @blk.vector
        def _(v):
            tA.wait(v)
            vA.inc(v.tensor_reduce(out=amax_sc[0:1, 0:1], in_=ps[5][0:1, 256:384],
                                   axis=AX, op=ALU.max))
            mk["amaxrd"] = vA.n

        @blk.gpsimd
        def _(g):
            vA.wait(g)
            dCC.dinc(g.dma_start(out=ccxi[0:1, :], in_=amax_sc[0:1, :]))
            dCC.wait(g)
            cCC.inc(g.collective_compute("AllReduce", ALU.max,
                                         replica_groups=[list(range(8))],
                                         ins=[ccxi.opt()], outs=[ccxo.opt()]))
            cCC.wait(g)
            dCC.dinc(g.dma_start(out=glob_sc[0:1, :], in_=ccxo[0:1, :]))

        gA.close()

        gB = ExitStack()
        qk_fm = gB.enter_context(nc.sbuf_tensor("qk_fm", [128, 16 * SC], BF16))
        vplus = gB.enter_context(nc.sbuf_tensor("vplus", [128, 4 * 1040], BF16))
        q2b = gB.enter_context(nc.sbuf_tensor("q2b", [128, 2 * SC], BF16))
        Ssb = gB.enter_context(nc.sbuf_tensor("Ssb", [16, 2 * SC], F32))
        nqk = gB.enter_context(nc.sbuf_tensor("nqk", [16, 2 * SC], FP16))
        n2sb = gB.enter_context(nc.sbuf_tensor("n2sb", [128, 2 * SC], FP16))
        argb = gB.enter_context(nc.sbuf_tensor("argb", [128, 2 * SC], F32))
        qf = gB.enter_context(nc.sbuf_tensor("qf", [128, 8 * SC], BF16))
        kf = gB.enter_context(nc.sbuf_tensor("kf", [128, 8 * SC], BF16))
        kf_tm = gB.enter_context(nc.sbuf_tensor("kf_tm", [128, 4 * 1024], BF16))
        stp = gB.enter_context(nc.sbuf_tensor("stp", [128, 8 * 65], BF16))
        mat = gB.enter_context(nc.sbuf_tensor("mat", [128, 1280], BF16))
        Ksb = gB.enter_context(nc.sbuf_tensor("Ksb", [64, 16], F32))
        S32 = gB.enter_context(nc.sbuf_tensor("S32", [64, 1024], F32))
        dcol = gB.enter_context(nc.sbuf_tensor("dcol", [128, 4], F32))
        rcol = gB.enter_context(nc.sbuf_tensor("rcol", [128, 4], F32))
        ctx_tm = gB.enter_context(nc.sbuf_tensor("ctx_tm", [128, 4 * 1024], BF16))
        ctxfm = gB.enter_context(nc.sbuf_tensor("ctxfm", [128, 8 * SC], BF16))

        bQKt = C("bQKt"); bQKs = C("bQKs"); bVt = C("bVt"); bVs = C("bVs")
        bSQs = C("bSQs"); bNMt = C("bNMt"); bNSv = C("bNSv"); bNs = C("bNs")
        bPJt = C("bPJt"); bN2t = C("bN2t"); bN2s = C("bN2s"); bTTv = C("bTTv")
        bCLg = C("bCLg"); bXPs = C("bXPs"); bC2v = C("bC2v")
        bKTt = C("bKTt"); bKTs = C("bKTs"); bSPs = C("bSPs")
        bATt = C("bATt"); bMDv = C("bMDv"); bMOs = C("bMOs")
        bCXt = C("bCXt"); bDEv = C("bDEv"); bUPt = C("bUPt")
        bTRt = C("bTRt"); bTRs = C("bTRs"); bKDv = C("bKDv"); bSTg = C("bSTg")



        @blk.vector
        def _(v):
            if BSTEP < 2:
                return
            tA.wait(v)
            sA.wait(v)
            vW.wait(v)
            dSP.wait(v)
            dWS.wait(v)
            dWr2[0].wait(v)
            dWr2[1].wait(v)
            dOr2[0].wait(v)
            dOr2[1].wait(v)
            vC.inc(v.memset(
                vplus.ap().rearrange("p (t h c) -> p t h c", h=16, c=65)[:, :, :, 64:65], 1.0))
            vC.inc(v.memset(stp[:], 0.0))
            vC.inc(v.memset(Ksb[:], 0.0))

        for sc in range(min(NSC, NSCRUN) if BSTEP >= 2 else 0):
            for m in range(16):
                @blk.tensor
                def _(t, sc=sc, m=m):
                    if m == 0:
                        sA.wait_abs(t, mk[("xnev", min(4 * sc + 3, NTT - 1))])
                        if sc == 0:
                            dW.wait(t)
                            vC.wait(t)
                        elif BSTEP >= 7:
                            bDEv.wait_abs(t, mk[("den", sc - 1, 15)])
                            bMDv.wait_abs(t, mk[("mask", sc - 1, 15)][0])
                            bMOs.wait_abs(t, mk[("mask", sc - 1, 15)][1])
                        elif BSTEP >= 5:
                            bTTv.wait_abs(t, mk[("tt", sc - 1, 15)])
                    bk = ps[m % 2]
                    if m >= 2:
                        bQKs.wait_abs(t, mk[("qke", sc, m - 2)])
                    for k in range(8):
                        ins = t.matmul(bk[:, 0:512],
                                       wq_sb[:, 3 * D * k + 128 * m:3 * D * k + 128 * (m + 1)],
                                       xn[:, 4096 * k + SC * sc:4096 * k + SC * (sc + 1)],
                                       start=(k == 0), stop=(k == 7))
                    bQKt.inc(ins)
                    mk[("qkm", sc, m)] = bQKt.n

                @blk.scalar
                def _(s, sc=sc, m=m):
                    bk = ps[m % 2]
                    bQKt.wait_abs(s, mk[("qkm", sc, m)])
                    if sc > 0 and m == 0 and BSTEP >= 5:
                        bPJt.wait_abs(s, mk[("pjall", sc - 1)])
                    bQKs.inc(s.activation(out=qk_fm[:, SC * m:SC * (m + 1)], in_=bk[:, 0:512],
                                          func=AF.Identity, bias=bq_sb[:, m:m + 1], scale=1.0))
                    mk[("qke", sc, m)] = bQKs.n

            for g8 in range(8 if BSTEP >= 3 else 0):
                @blk.tensor
                def _(t, sc=sc, g8=g8):
                    ts4, g = g8 // 2, g8 % 2
                    bk = ps[2 + g8 % 2]
                    if sc > 0 and g8 < 2 and BSTEP >= 5:
                        bN2s.wait_abs(t, mk[("n2c", sc - 1, 14 + g8)])
                    if (sc, g8) not in [(0, 0), (0, 1)]:
                        key = ("ve", sc if g8 >= 2 else sc - 1, g8 - 2 if g8 >= 2 else 6 + g8)
                        bVs.wait_abs(t, mk[key])
                    for k in range(8):
                        ins = t.matmul(bk[:, 0:512],
                                       xn[:, 4096 * k + SC * sc + 128 * ts4:4096 * k + SC * sc + 128 * (ts4 + 1)],
                                       wq_sb[:, 3 * D * k + 2048 + 512 * g:3 * D * k + 2048 + 512 * (g + 1)],
                                       start=(k == 0), stop=(k == 7))
                    bVt.inc(ins)
                    mk[("vm", sc, g8)] = bVt.n

                @blk.scalar
                def _(s, sc=sc, g8=g8):
                    ts4, g = g8 // 2, g8 % 2
                    bk = ps[2 + g8 % 2]
                    bVt.wait_abs(s, mk[("vm", sc, g8)])
                    if sc > 0 and g8 == 0 and BSTEP >= 7:
                        bCXt.wait_abs(s, mk[("cxall", sc - 1)])
                        bUPt.wait_abs(s, mk[("upall", sc - 1)])
                    if sc == 0 and g8 == 0:
                        vC.wait(s)
                    dst = vplus.ap().rearrange("p (t h c) -> p t h c", h=16, c=65)[
                        :, ts4, 8 * g:8 * (g + 1), 0:64]
                    bVs.inc(s.activation(out=dst,
                                         in_=bk.ap().rearrange("p (h c) -> p h c", c=64),
                                         func=AF.Identity, bias=zero_col[:], scale=1.0))
                    mk[("ve", sc, g8)] = bVs.n

            for c in range(16 if BSTEP >= 4 else 0):
                @blk.scalar
                def _(s, sc=sc, c=c):
                    if c >= 2:
                        bNMt.wait_abs(s, mk[("nm", sc, c - 2)])
                    bSQs.inc(s.activation(out=q2b[:, SC * (c % 2):SC * (c % 2 + 1)],
                                          in_=qk_fm[:, SC * c:SC * (c + 1)],
                                          func=AF.Square, bias=zero_col[:], scale=1.0))
                    mk[("sq", sc, c)] = bSQs.n

                @blk.tensor
                def _(t, sc=sc, c=c):
                    if c == 0:
                        bVs.wait_abs(t, mk[("ve", sc, 7)])
                        if sc > 0:
                            bNSv.wait_abs(t, mk[("nsv", sc - 1)])
                    bk = ps[2] if c < 8 else ps[3]
                    bSQs.wait_abs(t, mk[("sq", sc, c)])
                    cc = c % 8
                    ins = t.matmul(bk[0:16, 0:512], m16_sb[:, 16 * cc:16 * (cc + 1)],
                                   q2b[:, SC * (c % 2):SC * (c % 2 + 1)],
                                   start=(cc == 0), stop=(cc == 7))
                    bNMt.inc(ins)
                    mk[("nm", sc, c)] = bNMt.n

            @blk.vector
            def _(v, sc=sc):
                if BSTEP < 4:
                    return
                bNMt.wait_abs(v, mk[("nm", sc, 7)])
                bNSv.inc(v.tensor_scalar(out=Ssb[0:16, 0:512], in0=ps[2][0:16, 0:512],
                                         scalar1=1e-24, scalar2=None, op0=ALU.max))
                bNMt.wait_abs(v, mk[("nm", sc, 15)])
                bNSv.inc(v.tensor_scalar(out=Ssb[0:16, 512:1024], in0=ps[3][0:16, 0:512],
                                         scalar1=1e-24, scalar2=None, op0=ALU.max))
                mk[("nsv", sc)] = bNSv.n

            @blk.scalar
            def _(s, sc=sc):
                if BSTEP < 4:
                    return
                bNSv.wait_abs(s, mk[("nsv", sc)])
                bNs.inc(s.activation(out=argb[0:16, 0:1024], in_=Ssb[0:16, 0:1024],
                                     func=AF.Ln, bias=zero_col[0:16, :], scale=1.0))
                bNs.wait(s)
                bNs.inc(s.activation(out=nqk[0:16, 0:1024], in_=argb[0:16, 0:1024],
                                     func=AF.Exp, bias=zero_col[0:16, :], scale=-0.5))
                mk[("n", sc)] = bNs.n

            for u16 in range(16 if BSTEP >= 5 else 0):
                isq = u16 < 8
                hp = u16 % 8
                pjbank = ps[u16 % 2]
                n2bank = ps[2 + u16 % 2]
                n2slot = u16 % 2
                argslot = u16 % 2

                @blk.tensor
                def _(t, sc=sc, u16=u16, isq=isq, hp=hp, pjbank=pjbank, n2bank=n2bank):
                    m = hp if isq else 8 + hp
                    bQKs.wait_abs(t, mk[("qke", sc, m)])
                    if (sc, u16) not in [(0, 0), (0, 1)]:
                        key = ("ttp", sc, u16 - 2) if u16 >= 2 else ("tt", sc - 1, 14 + u16)
                        bTTv.wait_abs(t, mk[key])
                    for u in range(2):
                        base = 64 * u
                        ins = t.matmul(pjbank[base:base + 64, 0:512],
                                       omg[base:base + 64, 64 * hp:64 * (hp + 1)],
                                       qk_fm[base:base + 64, SC * m:SC * (m + 1)],
                                       start=True, stop=True,
                                       tile_position=(base, base))
                    bPJt.inc(ins)
                    mk[("pj", sc, u16)] = bPJt.n
                    if u16 == 15:
                        mk[("pjall", sc)] = bPJt.n
                    if S9SUB < 2:
                        mk[("n2", sc, u16)] = bN2t.n
                        return
                    bNs.wait_abs(t, mk[("n", sc)])
                    if (sc, u16) not in [(0, 0), (0, 1)]:
                        key = ("n2c", sc, u16 - 2) if u16 >= 2 else ("n2c", sc - 1, 14 + u16)
                        bN2s.wait_abs(t, mk[key])
                    off = 0 if isq else 512
                    bN2t.inc(t.matmul(n2bank[:, 0:512], ihpq_sb[:, 128 * hp:128 * (hp + 1)],
                                      nqk[0:16, off:off + 512], start=True, stop=True))
                    mk[("n2", sc, u16)] = bN2t.n

                @blk.scalar
                def _(s, sc=sc, u16=u16, n2bank=n2bank, n2slot=n2slot):
                    if S9SUB < 3:
                        mk[("n2c", sc, u16)] = bN2s.n
                        return
                    bN2t.wait_abs(s, mk[("n2", sc, u16)])
                    if (sc, u16) not in [(0, 0), (0, 1)]:
                        key = ("ttp", sc, u16 - 2) if u16 >= 2 else ("tt", sc - 1, 14 + u16)
                        bTTv.wait_abs(s, mk[key])
                    bN2s.inc(s.activation(out=n2sb[:, SC * n2slot:SC * (n2slot + 1)],
                                          in_=n2bank[:, 0:512], func=AF.Identity,
                                          bias=zero_col[:], scale=1.0))
                    mk[("n2c", sc, u16)] = bN2s.n

                @blk.vector
                def _(v, sc=sc, u16=u16, pjbank=pjbank, n2slot=n2slot, argslot=argslot):
                    if S9SUB < 4:
                        mk[("tt", sc, u16)] = bTTv.n
                        return
                    bN2s.wait_abs(v, mk[("n2c", sc, u16)])
                    bPJt.wait_abs(v, mk[("pj", sc, u16)])
                    if (sc, u16) not in [(0, 0), (0, 1)]:
                        if u16 >= 2:
                            if S9SUB >= 6:
                                bXPs.wait_abs(v, mk[("xp", sc, u16 - 2)])
                            elif S9SUB >= 5:
                                bCLg.wait_abs(v, mk[("clip", sc, u16 - 2)])
                        else:
                            if S9SUB >= 6:
                                bXPs.wait_abs(v, mk[("xp", sc - 1, 14 + u16)])
                            elif S9SUB >= 5:
                                bCLg.wait_abs(v, mk[("clip", sc - 1, 14 + u16)])
                    a = argb[:, SC * argslot:SC * (argslot + 1)]
                    bTTv.inc(v.tensor_tensor(out=a, in0=pjbank[:, 0:512],
                                             in1=n2sb[:, SC * n2slot:SC * (n2slot + 1)],
                                             op=ALU.mult))
                    mk[("ttp", sc, u16)] = bTTv.n
                    bTTv.wait(v)
                    bTTv.inc(v.tensor_scalar(out=a, in0=a, scalar1=negs_col[:, 0:1],
                                             scalar2=20.0, op0=ALU.add, op1=ALU.min))
                    bTTv.wait(v)
                    bTTv.inc(v.tensor_scalar(out=a, in0=a, scalar1=-20.0, scalar2=None,
                                             op0=ALU.max))
                    mk[("tt", sc, u16)] = bTTv.n
                    mk[("clip", sc, u16)] = bCLg.n

                def _xp(s, sc=sc, ux=None):
                    isqx = ux < 8
                    hpx = ux % 8
                    if S9SUB < 6:
                        mk[("xp", sc, ux)] = bXPs.n
                        return
                    bTTv.wait_abs(s, mk[("tt", sc, ux)])
                    dstt = qf if isqx else kf
                    if sc > 0 and ux == 0 and BSTEP >= 7:
                        bATt.wait_abs(s, mk[("atall", sc - 1)])
                        bCXt.wait_abs(s, mk[("cxall", sc - 1)])
                    if sc > 0 and ux == 8 and BSTEP >= 6:
                        bKTt.wait_abs(s, mk[("ktall", sc - 1)])
                    bXPs.inc(s.activation(out=dstt[:, SC * hpx:SC * (hpx + 1)],
                                          in_=argb[:, SC * (ux % 2):SC * (ux % 2 + 1)],
                                          func=AF.Exp,
                                          bias=zero_col[:] if isqx else lnc2_col[:],
                                          scale=1.0))
                    mk[("xp", sc, ux)] = bXPs.n

                if u16 >= 1:
                    @blk.scalar
                    def _(s, sc=sc, u16=u16, _xp=_xp):
                        _xp(s, sc=sc, ux=u16 - 1)
                if u16 == 15:
                    @blk.scalar
                    def _(s, sc=sc, _xp=_xp):
                        _xp(s, sc=sc, ux=15)

                if not isq:
                    mk[("c2", sc, hp)] = ("xp", sc, u16)

            for hp in range(8 if BSTEP >= 6 else 0):
                @blk.tensor
                def _(t, sc=sc, hp=hp):
                    bXPs.wait_abs(t, mk[mk[("c2", sc, hp)]])
                    pb = psb[hp % 2]
                    if sc > 0 and hp < 2 and BSTEP >= 8:
                        bTRs.wait_abs(t, mk[("tre", sc - 1, 2 + hp)])
                    if (sc, hp) not in [(0, 0), (0, 1)]:
                        key = ("kte", sc, hp - 2) if hp >= 2 else ("kte", sc - 1, 6 + hp)
                        bKTs.wait_abs(t, mk[key])
                    last = None
                    for ts4 in range(4):
                        last = t.transpose(pb[:, 128 * ts4:128 * (ts4 + 1)],
                                           kf[:, SC * hp + 128 * ts4:SC * hp + 128 * (ts4 + 1)],
                                           eyeb_sb[:])
                    bKTt.inc(last)
                    mk[("kt", sc, hp)] = bKTt.n
                    if hp == 7:
                        mk[("ktall", sc)] = bKTt.n

                @blk.scalar
                def _(s, sc=sc, hp=hp):
                    bKTt.wait_abs(s, mk[("kt", sc, hp)])
                    if sc > 0 and hp == 0 and BSTEP >= 7:
                        bUPt.wait_abs(s, mk[("upall", sc - 1)])
                    pb = psb[hp % 2]
                    dst = kf_tm.ap().rearrange("p (t c) -> p t c", c=1024)[
                        :, :, 128 * hp:128 * (hp + 1)]
                    bKTs.inc(s.activation(out=dst,
                                          in_=pb.ap().rearrange("p (t c) -> p t c", c=128)[:, 0:4, :],
                                          func=AF.Identity, bias=zero_col[:], scale=1.0))
                    mk[("kte", sc, hp)] = bKTs.n

            @blk.scalar
            def _(s, sc=sc):
                if BSTEP < 7:
                    return
                if sc > 0:
                    vH.wait_abs(s, mk[("s32", sc - 1, 0)])
                    vH.wait_abs(s, mk[("s32", sc - 1, 1)])
                    bKDv.wait_abs(s, mk[("kdall", sc - 1)])
                    bCXt.wait_abs(s, mk[("cxall", sc - 1)])
                    bSPs.inc(s.activation(
                        out=stp.ap().rearrange("p (b c) -> p b c", c=65)[0:64, :, 0:64],
                        in_=S32.ap().rearrange("p (b c) -> p b c", c=64)[:, 0:8, :],
                        func=AF.Identity, bias=zero_col[0:64, :], scale=1.0))
                    bSPs.inc(s.activation(
                        out=stp.ap().rearrange("p (b c) -> p b c", c=65)[0:64, :, 64:65],
                        in_=Ksb.ap().rearrange("p (b c) -> p b c", c=2)[:, :, 0:1],
                        func=AF.Identity, bias=zero_col[0:64, :], scale=1.0))
                mk[("stpcp", sc)] = bSPs.n

            @blk.gpsimd
            def _(g, sc=sc):
                if BSTEP < 7:
                    return
                if sc > 0:
                    vH.wait_abs(g, mk[("s32", sc - 1, 1)])
                    bKDv.wait_abs(g, mk[("kdall", sc - 1)])
                    bCXt.wait_abs(g, mk[("cxall", sc - 1)])
                    bSTg.dinc(g.dma_start(
                        out=stp.ap().rearrange("p (b c) -> p b c", c=65)[64:128, :, 0:64],
                        in_=S32.ap().rearrange("p (b c) -> p b c", c=64)[:, 8:16, :]))
                    bSTg.dinc(g.dma_start(
                        out=stp.ap().rearrange("p (b c) -> p b c", c=65)[64:128, :, 64:65],
                        in_=Ksb.ap().rearrange("p (b c) -> p b c", c=2)[:, :, 1:2],
                        allow_slow_non_contiguous=True))
                mk[("stg", sc)] = bSTg.n

            @blk.scalar
            def _(s, sc=sc):
                if BSTEP < 7:
                    mk[("stp", sc)] = bSPs.n
                    return
                if sc > 0:
                    bSTg.wait_abs(s, mk[("stg", sc)])
                    bSPs.inc(s.nop())
                mk[("stp", sc)] = bSPs.n

            for h in range(16 if BSTEP >= 7 else 0):
                hp, u = h // 2, h % 2
                base = 64 * u
                atbank = ps[2 + h % 2]
                cxbank = ps[h % 2]

                for kc in range(4):
                    @blk.tensor
                    def _(t, sc=sc, h=h, hp=hp, u=u, base=base, atbank=atbank, kc=kc):
                        if kc == 0:
                            bXPs.wait_abs(t, mk[("xp", sc, hp)])
                            bXPs.wait_abs(t, mk[mk[("c2", sc, hp)]])
                            if h < 2:
                                bN2s.wait_abs(t, mk[("n2c", sc, 14 + h)])
                            if (sc, h) != (0, 0):
                                keyc = ("cx", sc, h - 1) if h >= 1 else ("cx", sc - 1, 15)
                                bCXt.wait_abs(t, mk[keyc])
                        prev = (sc, h, kc - 2)
                        if kc >= 2:
                            bMDv.wait_abs(t, mk[("maskd1", sc, h, kc - 2)])
                            if kc - 2 < 3:
                                bMOs.wait_abs(t, mk[("masko1", sc, h, kc - 2)])
                        elif (sc, h) not in [(0, 0), (0, 1)]:
                            key = (sc, h - 2) if h >= 2 else (sc - 1, 14 + h)
                            bMDv.wait_abs(t, mk[("maskd1", key[0], key[1], kc + 2)])
                            if kc + 2 < 3:
                                bMOs.wait_abs(t, mk[("masko1", key[0], key[1], kc + 2)])
                            bMDv.wait_abs(t, mk[("maskd1", key[0], key[1], kc)])
                            if kc < 3:
                                bMOs.wait_abs(t, mk[("masko1", key[0], key[1], kc)])
                        nkc = 512 - 128 * kc
                        bATt.inc(t.matmul(
                            atbank[:, 0:nkc] if kc % 2 == 0 else ps[2 + (h + 1) % 2][:, 0:nkc],
                            kf[base:base + 64, SC * hp + 128 * kc:SC * hp + 128 * (kc + 1)],
                            qf[base:base + 64, SC * hp + 128 * kc:SC * (hp + 1)],
                            start=True, stop=True, tile_position=(base, 0)))
                        mk[("at", sc, h, kc)] = bATt.n
                        if h == 15 and kc == 3:
                            mk[("atall", sc)] = bATt.n

                    @blk.vector
                    def _(v, sc=sc, h=h, atbank=atbank, kc=kc):
                        bATt.wait_abs(v, mk[("at", sc, h, kc)])
                        bk = atbank if kc % 2 == 0 else ps[2 + (h + 1) % 2]
                        bMDv.inc(v.tensor_tensor(out=mat[:, MOFF[kc]:MOFF[kc] + 128],
                                                 in0=bk[:, 0:128], in1=tri_sb[:],
                                                 op=ALU.mult))
                        mk[("maskd1", sc, h, kc)] = bMDv.n

                    @blk.scalar
                    def _(s, sc=sc, h=h, atbank=atbank, kc=kc):
                        if kc < 3:
                            nkc = 512 - 128 * kc
                            bATt.wait_abs(s, mk[("at", sc, h, kc)])
                            bMDv.wait_abs(s, mk[("maskd1", sc, h, kc)])
                            bk = atbank if kc % 2 == 0 else ps[2 + (h + 1) % 2]
                            bMOs.inc(s.activation(out=mat[:, MOFF[kc] + 128:MOFF[kc] + nkc],
                                                  in_=bk[:, 128:nkc], func=AF.Identity,
                                                  bias=zero_col[:], scale=1.0))
                            mk[("masko1", sc, h, kc)] = bMOs.n
                        if kc == 3:
                            mk[("mask", sc, h)] = (bMDv.n, bMOs.n)

                @blk.tensor
                def _(t, sc=sc, h=h, hp=hp, u=u, base=base, cxbank=cxbank):
                    if S12SUB < 2:
                        mk[("cx", sc, h)] = bCXt.n
                        if h == 15:
                            mk[("cxall", sc)] = bCXt.n
                        return
                    bMDv.wait_abs(t, mk[("mask", sc, h)][0])
                    bMOs.wait_abs(t, mk[("mask", sc, h)][1])
                    bSPs.wait_abs(t, mk[("stp", sc)])
                    if h < 2:
                        bTTv.wait_abs(t, mk[("ttp", sc, 14 + h)])
                    if (sc, h) not in [(0, 0), (0, 1)]:
                        key = ("den", sc, h - 2) if h >= 2 else ("den", sc - 1, 14 + h)
                        bDEv.wait_abs(t, mk[key])
                    bKTs.wait_abs(t, mk[("kte", sc, hp)])
                    for qs in range(4):
                        if qs == 3:
                            for kc in range(4):
                                bCXt.inc(t.matmul(
                                    cxbank[0:64, 260:261],
                                    kf_tm[:, 1024 * kc + 128 * hp + base:1024 * kc + 128 * hp + base + 64],
                                    vplus[:, 1040 * kc + 65 * h + 64:1040 * kc + 65 * (h + 1)],
                                    start=False, stop=False))
                        for kc in range(qs + 1):
                            last = (qs == 3 and ((sc == 0 and kc == 3)))
                            bCXt.inc(t.matmul(
                                cxbank[:, 65 * qs:65 * (qs + 1)],
                                mat[:, MOFF[kc] + 128 * (qs - kc):MOFF[kc] + 128 * (qs - kc + 1)],
                                vplus[:, 1040 * kc + 65 * h:1040 * kc + 65 * (h + 1)],
                                start=(qs == 0 and kc == 0), stop=last))
                        if sc > 0:
                            bCXt.inc(t.matmul(
                                cxbank[:, 65 * qs:65 * (qs + 1)],
                                qf[base:base + 64, SC * hp + 128 * qs:SC * hp + 128 * (qs + 1)],
                                stp[base:base + 64, 65 * hp:65 * (hp + 1)],
                                start=False, stop=(qs == 3), tile_position=(base, 0)))
                    mk[("cx", sc, h)] = bCXt.n
                    if h == 15:
                        mk[("cxall", sc)] = bCXt.n

                def _den(dh, sc=sc):
                    @blk.vector
                    def _(v, sc=sc, h=dh, hp=dh // 2, u=dh % 2, cxbank=ps[dh % 2]):
                        if S12SUB < 2:
                            mk[("den", sc, h)] = bDEv.n
                            if h == 15:
                                mk[("kdall", sc)] = bKDv.n
                            return
                        bCXt.wait_abs(v, mk[("cx", sc, h)])
                        if sc > 0:
                            bKDv.inc(v.tensor_tensor(out=Ksb[:, 2 * hp + u:2 * hp + u + 1],
                                                     in0=Ksb[:, 2 * hp + u:2 * hp + u + 1],
                                                     in1=cxbank[0:64, 260:261], op=ALU.add))
                        else:
                            bKDv.inc(v.tensor_copy(Ksb[:, 2 * hp + u:2 * hp + u + 1],
                                                   cxbank[0:64, 260:261]))
                        if h == 15:
                            mk[("kdall", sc)] = bKDv.n
                        bDEv.inc(v.tensor_scalar(
                            out=dcol.ap().rearrange("p (q c) -> p q c", c=1),
                            in0=cxbank.ap()[:, 0:260].rearrange("p (q c) -> p q c", c=65)[:, :, 64:65],
                            scalar1=1e-6, scalar2=None, op0=ALU.max))
                        bDEv.wait(v)
                        bDEv.inc(v.reciprocal(rcol[:, 0:4], dcol[:, 0:4]))
                        bDEv.wait(v)
                        for qs in range(4):
                            bDEv.inc(v.scalar_tensor_tensor(
                                out=ctx_tm[:, 1024 * qs + 64 * h:1024 * qs + 64 * (h + 1)],
                                in0=cxbank[:, 65 * qs:65 * qs + 64], scalar=rcol[:, qs:qs + 1],
                                in1=BV[:, 64 * h:64 * (h + 1)], op0=ALU.mult, op1=ALU.add))
                        mk[("den", sc, h)] = bDEv.n


                if h >= 1:
                    _den(h - 1)
                if h == 15:
                    _den(15)

                @blk.tensor
                def _(t, sc=sc, h=h, hp=hp, u=u, base=base):
                    if S12SUB < 3:
                        mk[("up", sc, h)] = bUPt.n
                        if h == 15:
                            mk[("upall", sc)] = bUPt.n
                        return
                    sbank = ps[4] if u == 0 else ps[5]
                    if sc == 0 and h == 1:
                        vA.wait_abs(t, mk["amaxrd"])
                    if h == u and sc > 0:
                        vH.wait_abs(t, mk[("s32", sc - 1, u)])
                    for kc in range(4):
                        bUPt.inc(t.matmul(
                            sbank[0:64, 64 * hp:64 * (hp + 1)],
                            kf_tm[:, 1024 * kc + 128 * hp + base:1024 * kc + 128 * hp + base + 64],
                            vplus[:, 1040 * kc + 65 * h:1040 * kc + 65 * h + 64],
                            start=(h == u and kc == 0), stop=(h == 14 + u and kc == 3)))
                    mk[("up", sc, h)] = bUPt.n
                    if h == 15:
                        mk[("upall", sc)] = bUPt.n

                if h >= 14:
                    @blk.vector
                    def _(v, sc=sc, h=h, u=h - 14):
                        if S12SUB < 3:
                            mk[("s32", sc, u)] = vH.n
                            return
                        bUPt.wait_abs(v, mk[("up", sc, h)])
                        sbank = ps[4] if u == 0 else ps[5]
                        if sc == 0:
                            vH.inc(v.tensor_copy(S32[:, 512 * u:512 * (u + 1)],
                                                 sbank[0:64, 0:512]))
                        else:
                            vH.inc(v.tensor_tensor(out=S32[:, 512 * u:512 * (u + 1)],
                                                   in0=S32[:, 512 * u:512 * (u + 1)],
                                                   in1=sbank[0:64, 0:512], op=ALU.add))
                        mk[("s32", sc, u)] = vH.n

            for qs in range(4 if BSTEP >= 8 else 0):
                @blk.tensor
                def _(t, sc=sc, qs=qs):
                    bDEv.wait_abs(t, mk[("den", sc, 15)])
                    bKTs.wait_abs(t, mk[("kte", sc, 7)])
                    pb = psb[qs % 2]
                    if (sc, qs) not in [(0, 0), (0, 1)]:
                        key = ("tre", sc, qs - 2) if qs >= 2 else ("tre", sc - 1, 2 + qs)
                        bTRs.wait_abs(t, mk[key])
                    last = None
                    for c in range(8):
                        last = t.transpose(pb[:, 128 * c:128 * (c + 1)],
                                           ctx_tm[:, 1024 * qs + 128 * c:1024 * qs + 128 * (c + 1)],
                                           eyeb_sb[:])
                    bTRt.inc(last)
                    mk[("tr", sc, qs)] = bTRt.n

                @blk.scalar
                def _(s, sc=sc, qs=qs):
                    bTRt.wait_abs(s, mk[("tr", sc, qs)])
                    if sc > 0 and qs == 0:
                        dSP.wait(s)
                    pb = psb[qs % 2]
                    dst = ctxfm.ap().rearrange("p (c w) -> p c w", w=512)[
                        :, :, 128 * qs:128 * (qs + 1)]
                    bTRs.inc(s.activation(out=dst,
                                          in_=pb.ap().rearrange("p (c w) -> p c w", w=128),
                                          func=AF.Identity, bias=zero_col[:], scale=1.0))
                    mk[("tre", sc, qs)] = bTRs.n

            @blk.sync
            def _(sync, sc=sc):
                if BSTEP < 8:
                    return
                bTRs.wait_abs(sync, mk[("tre", sc, 3)])
                dSP.wait(sync)
                dSP.dinc(sync.dma_start(
                    out=ctx_spill.rearrange("(c p) t -> p c t", p=128)[:, :, SC * sc:SC * (sc + 1)],
                    in_=ctxfm.ap().rearrange("p (c w) -> p c w", w=512)))

            mk[("trall", sc)] = None

        if "qkfm" in dbg_out and BSTEP >= 2:
            @blk.sync
            def _(sync):
                bQKs.wait(sync)
                dST.dinc(sync.dma_start(out=dbg_out["qkfm"][:, :], in_=qk_fm[:]))
        if "qf" in dbg_out and BSTEP >= 5:
            @blk.sync
            def _(sync):
                bXPs.wait(sync)
                dST.dinc(sync.dma_start(out=dbg_out["qf"][:, :], in_=qf[:]))
        if "kf" in dbg_out and BSTEP >= 5:
            @blk.sync
            def _(sync):
                bC2v.wait(sync)
                dST.dinc(sync.dma_start(out=dbg_out["kf"][:, :], in_=kf[:]))
        if "vplus" in dbg_out and BSTEP >= 3:
            @blk.sync
            def _(sync):
                bVs.wait(sync)
                dST.dinc(sync.dma_start(out=dbg_out["vplus"][:, :], in_=vplus[:]))
        if "ctx" in dbg_out and BSTEP >= 7:
            @blk.sync
            def _(sync):
                bDEv.wait(sync)
                dST.dinc(sync.dma_start(out=dbg_out["ctx"][:, :], in_=ctx_tm[:]))

        gB.close()
        gBc.close()

        gC = ExitStack()
        wstg = gC.enter_context(nc.sbuf_tensor("wstg", [128, 2 * D], F32))
        wamax = gC.enter_context(nc.sbuf_tensor("wamax", [128, 8], F32))
        itile = gC.enter_context(nc.sbuf_tensor("itile", [128, 2 * D], I32))
        wsimt = gC.enter_context(nc.sbuf_tensor("wsimt", [128, 2 * D], BF16))
        win_sim = gC.enter_context(nc.sbuf_tensor("win_sim", [128, 8 * DFF], BF16))
        hstg = gC.enter_context(nc.sbuf_tensor("hstg", [128, 2 * T], BF16))
        hamax_cols = gC.enter_context(nc.sbuf_tensor("hamax_cols", [128, 40], F32))

        cWQs = sW; cWQv = vW; cWTt = tW; cWEs = sQ; cXQs = sF; cXQv = vQ
        cSXv = vX; cHt = tH; cHs = sH; cHD = dSP; cAMv = vH; cAMt = tA

        if BSTEP >= 9:
            @blk.tensor
            def _(t):
                dCC.wait(t)
                tC.inc(t.matmul(ps[0][:, 0:1], ones_row[0:1, :], glob_sc[0:1, 0:1],
                                start=True, stop=True))
                mk["sxbc"] = tC.n

            @blk.vector
            def _(v):
                tC.wait_abs(v, mk["sxbc"])
                cSXv.inc(v.tensor_scalar(out=sxi[:, 4:5], in0=ps[0][:, 0:1],
                                         scalar1=1e-8, scalar2=None, op0=ALU.max))
                cSXv.wait(v)
                cSXv.inc(v.reciprocal(sxi[:, 5:6], sxi[:, 4:5]))
                cSXv.wait(v)
                cSXv.inc(v.tensor_scalar(out=sxi[:, 0:1], in0=sxi[:, 5:6], scalar1=32767.0,
                                         scalar2=None, op0=ALU.mult))
                cSXv.inc(v.tensor_scalar(out=sxi[:, 1:2], in0=sxi[:, 4:5],
                                         scalar1=1.0 / 32767.0, scalar2=None, op0=ALU.mult))
                mk["sx"] = cSXv.n

            for jc in range(32):
                @blk.sync
                def _(sync, jc=jc):
                    if jc == 0:
                        bDEv.wait(sync); bUPt.wait(sync); vH.wait(sync)
                        bKDv.wait(sync); bTRs.wait(sync); bMOs.wait(sync)
                        bMDv.wait(sync); bXPs.wait(sync); bC2v.wait(sync)
                        bSPs.wait(sync); bSTg.wait(sync); dSP.wait(sync)
                        bQKs.wait(sync); bVs.wait(sync); bKTs.wait(sync)
                        bN2s.wait(sync); bNs.wait(sync); bSQs.wait(sync)
                        bTTv.wait(sync); bCLg.wait(sync); bNSv.wait(sync)
                        gF.wait(sync); vC.wait(sync)
                    if jc >= 2:
                        cWQs.wait_abs(sync, mk[("wi1", jc - 2)])
                        cWQv.wait_abs(sync, mk[("wi2", jc - 2)])
                    dIr2[jc % 2].dinc(sync.dma_start(
                        out=wstg[:, D * (jc % 2):D * (jc % 2 + 1)],
                        in_=win_t[128 * jc:128 * (jc + 1), :]))
                    mk[("wld", jc)] = dIr2[jc % 2].n

                @blk.vector
                def _(v, jc=jc):
                    sl = jc % 2
                    dIr2[sl].wait_abs(v, mk[("wld", jc)])
                    cWQv.inc(v.tensor_reduce(out=wamax[:, sl:sl + 1],
                                             in_=wstg[:, D * sl:D * (sl + 1)], axis=AX,
                                             op=ALU.max, apply_absolute_value=True))
                    cWQv.wait(v)
                    cWQv.inc(v.tensor_scalar(out=wamax[:, sl:sl + 1], in0=wamax[:, sl:sl + 1],
                                             scalar1=1e-8, scalar2=None, op0=ALU.max))
                    cWQv.wait(v)
                    cWQv.inc(v.reciprocal(wamax[:, 4 + sl:5 + sl], wamax[:, sl:sl + 1]))
                    cWQv.wait(v)
                    cWQv.inc(v.tensor_scalar(out=wamax[:, 2 + sl:3 + sl],
                                             in0=wamax[:, 4 + sl:5 + sl], scalar1=127.0,
                                             scalar2=None, op0=ALU.mult))
                    cWQv.inc(v.tensor_scalar(out=wamax[:, 6 + sl:7 + sl],
                                             in0=wamax[:, sl:sl + 1], scalar1=1.0 / 127.0,
                                             scalar2=None, op0=ALU.mult))
                    mk[("wsc", jc)] = cWQv.n

                @blk.scalar
                def _(s, jc=jc):
                    sl = jc % 2
                    cWQv.wait_abs(s, mk[("wsc", jc)])
                    if jc >= 2:
                        cWQv.wait_abs(s, mk[("wi2", jc - 2)])
                    cWQs.inc(s.activation(out=itile[:, D * sl:D * (sl + 1)],
                                          in_=wstg[:, D * sl:D * (sl + 1)],
                                          func=AF.Identity, bias=zero_col[:],
                                          scale=wamax[:, 2 + sl:3 + sl]))
                    mk[("wi1", jc)] = cWQs.n

                @blk.vector
                def _(v, jc=jc):
                    sl = jc % 2
                    cWQs.wait_abs(v, mk[("wi1", jc)])
                    if jc >= 2:
                        cWTt.wait_abs(v, mk[("wtt", jc - 2)])
                    cWQv.inc(v.tensor_scalar(out=wsimt[:, D * sl:D * (sl + 1)],
                                             in0=itile[:, D * sl:D * (sl + 1)],
                                             scalar1=wamax[:, 6 + sl:7 + sl], scalar2=None,
                                             op0=ALU.mult))
                    mk[("wi2", jc)] = cWQv.n

                @blk.tensor
                def _(t, jc=jc):
                    sl = jc % 2
                    cWQv.wait_abs(t, mk[("wi2", jc)])
                    if jc >= 2:
                        cWEs.wait_abs(t, mk[("wte", jc - 2)])
                    pb = psb[jc % 2]
                    last = None
                    for ib in range(8):
                        last = t.transpose(pb[:, 128 * ib:128 * (ib + 1)],
                                           wsimt[:, D * sl + 128 * ib:D * sl + 128 * (ib + 1)],
                                           eyeb_sb[:])
                    cWTt.inc(last)
                    mk[("wtt", jc)] = cWTt.n

                @blk.scalar
                def _(s, jc=jc):
                    cWTt.wait_abs(s, mk[("wtt", jc)])
                    pb = psb[jc % 2]
                    dst = win_sim.ap().rearrange("p (k a) -> p k a", a=DFF)[
                        :, :, 128 * jc:128 * (jc + 1)]
                    cWEs.inc(s.activation(out=dst,
                                          in_=pb.ap().rearrange("p (k a) -> p k a", a=128),
                                          func=AF.Identity, bias=zero_col[:], scale=1.0))
                    mk[("wte", jc)] = cWEs.n

            for pc in range(32):
                kc, q4 = pc // 4, pc % 4
                @blk.scalar
                def _(s, pc=pc, kc=kc, q4=q4):
                    if pc == 0:
                        load_act_set(s, 10)
                    cSXv.wait_abs(s, mk["sx"])
                    if pc >= 2:
                        cXQv.wait_abs(s, mk[("xq2", pc - 2)])
                    sl = pc % 2
                    cXQs.inc(s.activation(out=itile[:, D * sl:D * (sl + 1)],
                                          in_=xn[:, 4096 * kc + D * q4:4096 * kc + D * (q4 + 1)],
                                          func=AF.Identity, bias=zero_col[:],
                                          scale=sxi[:, 0:1]))
                    mk[("xq1", pc)] = cXQs.n

                @blk.vector
                def _(v, pc=pc, kc=kc, q4=q4):
                    cXQs.wait_abs(v, mk[("xq1", pc)])
                    sl = pc % 2
                    cXQv.inc(v.tensor_scalar(out=xn[:, 4096 * kc + D * q4:4096 * kc + D * (q4 + 1)],
                                             in0=itile[:, D * sl:D * (sl + 1)],
                                             scalar1=sxi[:, 1:2], scalar2=None, op0=ALU.mult))
                    mk[("xq2", pc)] = cXQv.n
                    if pc == 31:
                        mk["xqall"] = cXQv.n

            for j in range(32):
                for tt in range(8):
                    g = 8 * j + tt
                    bk = ps[g % 6]

                    @blk.tensor
                    def _(t, j=j, tt=tt, g=g, bk=bk):
                        if g == 0:
                            cXQv.wait_abs(t, mk["xqall"])
                        if g >= 6:
                            cHs.wait_abs(t, mk[("hev", g - 6)])
                        kwait = mk.get(("wte", 31))
                        if g == 0:
                            cWEs.wait_abs(t, kwait)
                        for k in range(8):
                            ins = t.matmul(bk[:, 0:512],
                                           win_sim[:, DFF * k + 128 * j:DFF * k + 128 * (j + 1)],
                                           xn[:, 4096 * k + 512 * tt:4096 * k + 512 * (tt + 1)],
                                           start=(k == 0), stop=(k == 7))
                        cHt.inc(ins)
                        mk[("hmm", g)] = cHt.n

                    @blk.scalar
                    def _(s, j=j, tt=tt, g=g, bk=bk):
                        cHt.wait_abs(s, mk[("hmm", g)])
                        if tt == 0 and j >= 2:
                            cHD.wait_abs(s, mk[("hsp", j - 2)])
                            cAMv.wait_abs(s, mk[("ham", j - 2)])
                        sl = j % 2
                        cHs.inc(s.activation(out=hstg[:, T * sl + 512 * tt:T * sl + 512 * (tt + 1)],
                                             in_=bk[:, 0:512], func=AF.Gelu,
                                             bias=bin_sb[:, j:j + 1], scale=1.0))
                        mk[("hev", g)] = cHs.n

                @blk.vector
                def _(v, j=j):
                    cHs.wait_abs(v, mk[("hev", 8 * j + 7)])
                    sl = j % 2
                    cAMv.inc(v.tensor_reduce(out=hamax_cols[:, j:j + 1],
                                             in_=hstg[:, T * sl:T * (sl + 1)], axis=AX,
                                             op=ALU.max, apply_absolute_value=True))
                    mk[("ham", j)] = cAMv.n

                @blk.sync
                def _(sync, j=j):
                    cHs.wait_abs(sync, mk[("hev", 8 * j + 7)])
                    cHD.wait(sync)
                    sl = j % 2
                    cHD.dinc(sync.dma_start(out=h_spill[128 * j:128 * (j + 1), :],
                                            in_=hstg[:, T * sl:T * (sl + 1)]))
                    mk[("hsp", j)] = cHD.n

            @blk.vector
            def _(v):
                cAMv.wait_abs(v, mk[("ham", 31)])
                cAMv.inc(v.tensor_reduce(out=hamax_cols[:, 36:37], in_=hamax_cols[:, 0:32],
                                         axis=AX, op=ALU.max))

            @blk.tensor
            def _(t):
                cAMv.wait(t)
                cHs.wait_abs(t, mk[("hev", 255)])
                cAMt.inc(t.transpose(ps[0][0:1, 0:128], hamax_cols[:, 36:37], eyef_sb[:]))

            @blk.vector
            def _(v):
                cAMt.wait(v)
                cAMv.inc(v.tensor_reduce(out=amax_sc[0:1, 0:1], in_=ps[0][0:1, 0:128],
                                         axis=AX, op=ALU.max))

            @blk.gpsimd
            def _(g):
                cAMv.wait(g)
                dCC2.dinc(g.dma_start(out=cchi[0:1, :], in_=amax_sc[0:1, :]))
                dCC2.wait(g)
                cCC2.inc(g.collective_compute("AllReduce", ALU.max,
                                              replica_groups=[list(range(8))],
                                              ins=[cchi.opt()], outs=[ccho.opt()]))
                cCC2.wait(g)
                dCC2.dinc(g.dma_start(out=globh_sc[0:1, :], in_=ccho[0:1, :]))

            if "wis" in dbg_out:
                @blk.sync
                def _(sync):
                    cWEs.wait(sync)
                    dST.dinc(sync.dma_start(out=dbg_out["wis"][:, :], in_=win_sim[:]))
            if "amxh" in dbg_out:
                @blk.sync
                def _(sync):
                    dCC2.wait(sync)
                    dST.dinc(sync.dma_start(out=dbg_out["amxh"][0:1, :], in_=globh_sc[0:1, :]))

        if BSTEP >= 10:
            gC.close()
            gXN.close()

            gD = ExitStack()
            wo_sim = gD.enter_context(nc.sbuf_tensor("wo_sim", [128, 32 * D], BF16))
            wo_att = gD.enter_context(nc.sbuf_tensor("wo_att", [128, 8 * D], BF16))
            itile2 = gD.enter_context(nc.sbuf_tensor("itile2", [128, 2 * D], I32))
            hslab = gD.enter_context(nc.sbuf_tensor("hslab", [128, 2 * 32 * 256], BF16))
            ctxsl = gD.enter_context(nc.sbuf_tensor("ctxsl", [128, 2 * 8 * 256], BF16))
            resf = gD.enter_context(nc.sbuf_tensor("resf", [128, 2 * 8 * 256], F32))
            xre = gD.enter_context(nc.sbuf_tensor("xre", [128, 2 * 2 * D], F32))
            otm = gD.enter_context(nc.sbuf_tensor("otm", [128, 2 * 2 * D], F32))

            dWA = dW; eSHv = vM; eHQs = sD; eHQv = vD
            eOt = tO; eOv = vO; eRt = tR; eRv = vA

            @blk.tensor
            def _(t):
                dCC2.wait(t)
                cAMt.wait(t)
                tC.inc(t.matmul(ps[0][:, 0:1], ones_row[0:1, :], globh_sc[0:1, 0:1],
                                start=True, stop=True))
                mk["shbc"] = tC.n

            @blk.vector
            def _(v):
                tC.wait_abs(v, mk["shbc"])
                eSHv.inc(v.tensor_scalar(out=sxi[:, 6:7], in0=ps[0][:, 0:1],
                                         scalar1=1e-8, scalar2=None, op0=ALU.max))
                eSHv.wait(v)
                eSHv.inc(v.reciprocal(sxi[:, 7:8], sxi[:, 6:7]))
                eSHv.wait(v)
                eSHv.inc(v.tensor_scalar(out=sxi[:, 2:3], in0=sxi[:, 7:8], scalar1=32767.0,
                                         scalar2=None, op0=ALU.mult))
                eSHv.inc(v.tensor_scalar(out=sxi[:, 3:4], in0=sxi[:, 6:7],
                                         scalar1=1.0 / 32767.0, scalar2=None, op0=ALU.mult))
                mk["sh"] = eSHv.n

            @blk.gpsimd
            def _(g):
                cHt.wait(g); cHs.wait(g); cHD.wait(g); cAMv.wait(g)
                cWEs.wait(g); cXQv.wait(g); cXQs.wait(g); cWQv.wait(g); cWTt.wait(g)
                dST.wait(g)
                dWA.dinc(g.dma_start(out=wo_att.ap().rearrange("p (c n) -> p c n", n=D),
                                     in_=wout.rearrange("(c p) n -> p c n", p=128)))

            @blk.sync
            def _(sync):
                cHt.wait(sync); cHs.wait(sync); cHD.wait(sync)
                cWEs.wait(sync); cXQv.wait(sync); cAMv.wait(sync)
                dST.wait(sync)
                dK.dinc(sync.dma_start(
                    out=wo_sim.ap().rearrange("p (c n) -> p c n", n=D),
                    in_=wos_spill.rearrange("(c p) n -> p c n", p=128)))
                mk["wosld"] = dK.n

            TT = 256

            def emit_residual(rt):
                rhf = rt % 2
                for q2 in range(2):
                    @blk.tensor
                    def _(t, t2=rt, q2=q2):
                        if q2 == 0:
                            eOv.wait_abs(t, mk[("oev", t2, 7)])
                        for half in range(2):
                            bk = ps[4 + half]
                            if q2 == 1:
                                eRv.wait_abs(t, mk[("rtv", t2, 0)])
                            last = None
                            for oo in range(4):
                                o = 4 * half + oo
                                last = t.transpose(bk[:, 128 * oo:128 * (oo + 1)],
                                                   resf[:, 2048 * (t2 % 2) + TT * o + 128 * q2:
                                                        2048 * (t2 % 2) + TT * o + 128 * (q2 + 1)],
                                                   eyef_sb[:])
                            eRt.inc(last)
                            mk[("rtt", t2, q2, half)] = eRt.n

                    @blk.vector
                    def _(v, t2=rt, hf=rhf, q2=q2):
                        dRL.wait_abs(v, mk[("dld", t2)])
                        if t2 >= 2 and q2 == 0:
                            dST.wait_abs(v, mk[("ost", t2 - 2)])
                        for half in range(2):
                            bk = ps[4 + half]
                            eRt.wait_abs(v, mk[("rtt", t2, q2, half)])
                            eRv.inc(v.scalar_tensor_tensor(
                                out=otm[:, 2 * D * hf + D * q2 + 512 * half:2 * D * hf + D * q2 + 512 * (half + 1)],
                                in0=bk[:, 0:512], scalar=1.0,
                                in1=xre[:, 2 * D * hf + D * q2 + 512 * half:2 * D * hf + D * q2 + 512 * (half + 1)],
                                op0=ALU.mult, op1=ALU.add))
                        mk[("rtv", t2, q2)] = eRv.n
                        if q2 == 1:
                            mk[("rev", t2)] = eRv.n

                @blk.sync
                def _(sync, t2=rt, hf=rhf):
                    eRv.wait_abs(sync, mk[("rev", t2)])
                    dST.wait(sync)
                    dST.dinc(sync.dma_start(
                        out=out.rearrange("(a p) c -> p a c", p=128)[:, 2 * t2:2 * (t2 + 1), :],
                        in_=otm.ap().rearrange("p (b a c) -> p b a c", b=2, c=D)[:, hf]))
                    mk[("ost", t2)] = dST.n

            for t2 in range(16):
                hf = t2 % 2

                @blk.sync
                def _(sync, t2=t2, hf=hf):
                    if t2 >= 1:
                        dRL.wait_abs(sync, mk[("dld", t2 - 1)])
                    if t2 >= 2:
                        eOt.wait_abs(sync, mk[("omm", t2 - 2, 7)])
                        eRv.wait_abs(sync, mk[("rev", t2 - 2)])
                    dRL.dinc(sync.dma_start(
                        out=hslab.ap().rearrange("p (b j w) -> p b j w", b=2, w=TT)[:, hf],
                        in_=h_spill.rearrange("(j p) t -> p j t", p=128)[:, :, TT * t2:TT * (t2 + 1)]))
                    dRL.dinc(sync.dma_start(
                        out=ctxsl.ap().rearrange("p (b c w) -> p b c w", b=2, w=TT)[:, hf],
                        in_=ctx_spill.rearrange("(c p) t -> p c t", p=128)[:, :, TT * t2:TT * (t2 + 1)]))
                    dRL.dinc(sync.dma_start(
                        out=xre.ap().rearrange("p (b a c) -> p b a c", b=2, c=D)[:, hf],
                        in_=x.rearrange("(a p) c -> p a c", p=128)[:, 2 * t2:2 * (t2 + 1), :]))
                    mk[("dld", t2)] = dRL.n

                for pc in range(8):
                    @blk.scalar
                    def _(s, t2=t2, hf=hf, pc=pc):
                        if pc == 0:
                            dRL.wait_abs(s, mk[("dld", t2)])
                            eSHv.wait_abs(s, mk["sh"])
                        if pc < 2 and t2 >= 1:
                            eHQv.wait_abs(s, mk[("hq2", t2 - 1, 6 + pc)])
                        if pc >= 2:
                            eHQv.wait_abs(s, mk[("hq2", t2, pc - 2)])
                        sl = pc % 2
                        eHQs.inc(s.activation(out=itile2[:, D * sl:D * (sl + 1)],
                                              in_=hslab[:, 8192 * hf + 1024 * pc:8192 * hf + 1024 * (pc + 1)],
                                              func=AF.Identity, bias=zero_col[:],
                                              scale=sxi[:, 2:3]))
                        mk[("hq1", t2, pc)] = eHQs.n

                    @blk.vector
                    def _(v, t2=t2, hf=hf, pc=pc):
                        eHQs.wait_abs(v, mk[("hq1", t2, pc)])
                        sl = pc % 2
                        eHQv.inc(v.tensor_scalar(out=hslab[:, 8192 * hf + 1024 * pc:8192 * hf + 1024 * (pc + 1)],
                                                 in0=itile2[:, D * sl:D * (sl + 1)],
                                                 scalar1=sxi[:, 3:4], scalar2=None,
                                                 op0=ALU.mult))
                        mk[("hq2", t2, pc)] = eHQv.n

                if t2 >= 1:
                    emit_residual(t2 - 1)

                for o in range(8):
                    bk = ps[o % 6]

                    @blk.tensor
                    def _(t, t2=t2, hf=hf, o=o, bk=bk):
                        if t2 == 0 and o == 0:
                            dK.wait_abs(t, mk["wosld"])
                            dWA.wait(t)
                            eSHv.wait_abs(t, mk["sh"])
                        go = 8 * t2 + o
                        if go >= 6:
                            pgo = go - 6
                            eOv.wait_abs(t, mk[("oev", pgo // 8, pgo % 8)])
                        if o in (4, 5) and t2 >= 1:
                            eRv.wait_abs(t, mk[("rtv", t2 - 1, 1)])
                        eHQv.wait_abs(t, mk[("hq2", t2, 7)])
                        dRL.wait_abs(t, mk[("dld", t2)])
                        for j in range(32):
                            t.matmul(bk[:, 0:TT],
                                     wo_sim[:, D * j + 128 * o:D * j + 128 * (o + 1)],
                                     hslab[:, 8192 * hf + TT * j:8192 * hf + TT * (j + 1)],
                                     start=(j == 0), stop=False)
                        for k in range(8):
                            ins = t.matmul(bk[:, 0:TT],
                                           wo_att[:, D * k + 128 * o:D * k + 128 * (o + 1)],
                                           ctxsl[:, 2048 * hf + TT * k:2048 * hf + TT * (k + 1)],
                                           start=False, stop=(k == 7))
                        eOt.inc(ins)
                        mk[("omm", t2, o)] = eOt.n

                    @blk.vector
                    def _(v, t2=t2, o=o, bk=bk):
                        eOt.wait_abs(v, mk[("omm", t2, o)])
                        if t2 >= 2:
                            eRt.wait_abs(v, mk[("rtt", t2 - 2, 1, 1)])
                        eOv.inc(v.tensor_scalar(out=resf[:, 2048 * (t2 % 2) + TT * o:2048 * (t2 % 2) + TT * (o + 1)],
                                                in0=bk[:, 0:TT], scalar1=bcomb[:, o:o + 1],
                                                scalar2=None, op0=ALU.add))
                        mk[("oev", t2, o)] = eOv.n

            emit_residual(15)

            if "wos" in dbg_out:
                @blk.sync
                def _(sync):
                    dK.wait_abs(sync, mk["wosld"])
                    dST.dinc(sync.dma_start(out=dbg_out["wos"][:, :], in_=wo_sim[:]))
            gD.close()

        if "xn" in dbg_out:
            @blk.sync
            def _(sync):
                sA.wait(sync)
                dST.dinc(sync.dma_start(out=dbg_out["xn"][:, :], in_=xn[:]))

        if "amx" in dbg_out:
            @blk.sync
            def _(sync):
                dCC.wait(sync)
                dST.dinc(sync.dma_start(out=dbg_out["amx"][0:1, :], in_=glob_sc[0:1, :]))

        if BSTEP < 9:
            gB.close()
            gBc.close()
        if 9 <= BSTEP < 10:
            gC.close()
        if BSTEP < 10:
            gXN.close()

        @blk.sync
        def _(sync):
            dST.wait(sync)
            dCC.wait(sync)
            vA.wait(sync)
            sA.wait(sync)

    return nc, dbg_out


def _prep_inputs(L0_attn_x, L0_attn_Wqkv, L0_attn_bqkv, L0_attn_Wout, L0_attn_bout,
                 L0_attn_omega, L0_attn_quad_nodes, L0_attn_quad_weights,
                 L0_ffn_W_in, L0_ffn_b_in, L0_ffn_W_out, L0_ffn_b_out):
    consts = host_consts()
    omega_t = np.ascontiguousarray(
        np.asarray(L0_attn_omega, dtype=np.float32).transpose(1, 2, 0, 3).reshape(H * HD, FD))
    win_t = np.ascontiguousarray(np.asarray(L0_ffn_W_in, dtype=np.float32).T)
    base = {
        "wqkv": np.ascontiguousarray(np.asarray(L0_attn_Wqkv, dtype=np.float32)),
        "bqkv": np.ascontiguousarray(np.asarray(L0_attn_bqkv, dtype=np.float32)),
        "wout": np.ascontiguousarray(np.asarray(L0_attn_Wout, dtype=np.float32)),
        "bout": np.ascontiguousarray(np.asarray(L0_attn_bout, dtype=np.float32)),
        "omega_t": omega_t,
        "qn": np.ascontiguousarray(np.asarray(L0_attn_quad_nodes, dtype=np.float32).reshape(2, 1)),
        "qw": np.ascontiguousarray(np.asarray(L0_attn_quad_weights, dtype=np.float32).reshape(2, 1)),
        "win_t": win_t,
        "b_in": np.ascontiguousarray(np.asarray(L0_ffn_b_in, dtype=np.float32)),
        "wofn": np.ascontiguousarray(np.asarray(L0_ffn_W_out, dtype=np.float32)),
        "b_out": np.ascontiguousarray(np.asarray(L0_ffn_b_out, dtype=np.float32)),
        "eye_f": consts["eye_f"], "eye_b": consts["eye_b"], "tri": consts["tri"],
        "m16": consts["m16"], "ihp": consts["ihp"], "sel2": consts["sel2"],
    }
    xs = np.asarray(L0_attn_x, dtype=np.float32)
    return [dict(base, x=np.ascontiguousarray(xs[b])) for b in range(8)]


def kernel(L0_attn_x, L0_attn_Wqkv, L0_attn_bqkv, L0_attn_Wout, L0_attn_bout,
           L0_attn_omega, L0_attn_quad_nodes, L0_attn_quad_weights,
           L0_ffn_W_in, L0_ffn_b_in, L0_ffn_W_out, L0_ffn_b_out,
           _dbg=(), _phases=4):
    in_maps = _prep_inputs(L0_attn_x, L0_attn_Wqkv, L0_attn_bqkv, L0_attn_Wout,
                           L0_attn_bout, L0_attn_omega, L0_attn_quad_nodes,
                           L0_attn_quad_weights, L0_ffn_W_in, L0_ffn_b_in,
                           L0_ffn_W_out, L0_ffn_b_out)
    nc, dbg_out = build_nc(dbg=_dbg, phases=_phases)
    res = run_bass_kernel_spmd(nc, in_maps, core_ids=list(range(8)))
    outs = np.stack([np.asarray(res.results[b]["out"]) for b in range(8)], axis=0)
    if _dbg:
        return outs, [res.results[b] for b in range(8)]
    return outs
